# version 02
# baseline (speedup 1.0000x reference)
import numpy as np
import ml_dtypes
import concourse.bass as bass
import concourse.mybir as mybir
from concourse.bass_utils import run_bass_kernel_spmd

F32 = mybir.dt.float32
BF16 = mybir.dt.bfloat16
I32 = mybir.dt.int32
AF = mybir.ActivationFunctionType
ALU = mybir.AluOpType
AX = mybir.AxisListType
BF = ml_dtypes.bfloat16

D = 2048
T = 4096
NCORES = 8
FC = D // 128
TT = 512
NTT = T // TT
CH = 256
NCH = T // CH
RH = 8
RD = D // RH
LH = 16
LD = D // LH
NB = T // 128
FF = 8192
EPS = 1e-5

ENGINES = ("sync", "scalar", "vector", "tensor", "gpsimd")


def _gammas():
    betas = np.linspace(np.log2(1.0 / 32.0), np.log2(1.0 / 512.0), RH)
    return (1.0 - np.exp2(betas)).astype(np.float64)


def host_consts():
    c = {}
    c["ident_bf"] = np.eye(128, dtype=np.float32).astype(BF)
    c["ident_f32"] = np.eye(128, dtype=np.float32)
    c["ones_row_bf"] = np.ones((1, TT), dtype=np.float32).astype(BF)
    c["ones128_row_bf"] = np.ones((1, 128), dtype=np.float32).astype(BF)
    c["ones_col_bf"] = np.ones((128, 1), dtype=np.float32).astype(BF)
    c["ones8_row_bf"] = np.ones((1, 8), dtype=np.float32).astype(BF)
    g8 = np.zeros((64, 8), dtype=np.float32)
    for k in range(64):
        g8[k, k // 8] = 1.0
    c["G8_bf"] = g8.astype(BF)
    c["E8_bf"] = np.ascontiguousarray(g8.T).astype(BF)
    c["C16"] = np.ascontiguousarray((np.arange(8, dtype=np.float32) + 16.0)[None, :].repeat(128, 0))
    c["cc"] = np.array([[EPS, 1.0 / D, 0.0, 1.0]], dtype=np.float32)
    g = _gammas()
    i = np.arange(CH, dtype=np.float64)
    dt_m = np.zeros((RH, CH, CH), dtype=np.float64)
    for h in range(RH):
        dt_m[h] = np.where(i[None, :] >= i[:, None], g[h] ** (i[None, :] - i[:, None]), 0.0)
    c["dt_bf"] = dt_m.astype(np.float32).astype(BF)
    c["cs_f32"] = np.stack([g[h] ** (i + 1) for h in range(RH)]).astype(np.float32).astype(BF)
    c["kv_f32"] = np.stack([g[h] ** (CH - 1 - i) for h in range(RH)]).astype(np.float32).astype(BF)
    c["iota64_bc"] = np.ascontiguousarray(np.arange(64, dtype=np.float32)[None, :].repeat(128, 0))
    c["iota128_bc"] = np.ascontiguousarray(np.arange(128, dtype=np.float32)[None, :].repeat(64, 0))
    be = np.zeros((128, 32), dtype=np.float32)
    be[0, 0] = 1.0
    be[127, 31] = 1.0
    c["bos_eos"] = be
    bt = np.zeros((32, 128), dtype=np.float32)
    bt[0, 0] = 1.0
    bt[31, 127] = 1.0
    c["beosT"] = bt
    us = np.triu(np.ones((32, 32), dtype=np.float32), 1)
    c["ustrict"] = us.astype(BF)
    c["ones64_row"] = np.ones((1, 64), dtype=np.float32).astype(BF)
    c["iota64_col"] = np.arange(64, dtype=np.float32)[:, None].copy()
    return c


def host_prep(inputs):
    x = np.asarray(inputs["L0_attn_x"])
    shared = {}
    for nm, key in (("l0wq", "L0_attn_Wq"), ("l0wk", "L0_attn_Wk"), ("l0wv", "L0_attn_Wv"),
                    ("l1wq", "L1_attn_Wq"), ("l1wk", "L1_attn_Wk"), ("l1wv", "L1_attn_Wv"),
                    ("l0bq", "L0_attn_bq"), ("l0bk", "L0_attn_bk"), ("l0bv", "L0_attn_bv"),
                    ("l1bq", "L1_attn_bq"), ("l1bk", "L1_attn_bk"), ("l1bv", "L1_attn_bv")):
        shared[nm] = np.ascontiguousarray(np.asarray(inputs[key], dtype=np.float32))
    sw = np.asarray(inputs["L0_ffn_select_weight"], dtype=np.float32)
    smd = np.asarray(inputs["L0_ffn_select_mean_diff"], dtype=np.float32)
    lu = np.asarray(inputs["L0_ffn_lora_update"], dtype=np.float32)
    ldg = np.asarray(inputs["L0_ffn_lora_degen"], dtype=np.float32)
    shared["sel_wall"] = np.ascontiguousarray(sw.transpose(1, 0, 2).reshape(D, 64))
    shared["sel_muT"] = np.ascontiguousarray(smd.T)
    shared["sel_mu"] = np.ascontiguousarray(smd)
    shared["sel_muexp"] = np.ascontiguousarray(np.repeat(smd, 8, axis=0).T)
    shared["sel_uall"] = np.ascontiguousarray(lu.transpose(1, 0, 2).reshape(D, 64))
    shared["sel_dbexp"] = np.ascontiguousarray(
        np.repeat(np.asarray(inputs["L0_ffn_debias_w"], dtype=np.float32), 8, axis=0).T)
    shared["sel_dg"] = np.ascontiguousarray(ldg.transpose(0, 2, 1).reshape(64, D))
    shared["sel_bias"] = np.ascontiguousarray(np.asarray(inputs["L0_ffn_bias_w"], dtype=np.float32))
    shared["imp_c"] = np.ascontiguousarray(np.asarray(inputs["L0_ffn_imp_center"], dtype=np.float32))
    shared["imp_s"] = np.ascontiguousarray(np.asarray(inputs["L0_ffn_imp_slope"], dtype=np.float32))
    shared["gw"] = np.ascontiguousarray(np.asarray(inputs["L1_ffn_gate_w"], dtype=np.int32))
    shared["uw"] = np.ascontiguousarray(np.asarray(inputs["L1_ffn_up_w"], dtype=np.int32))
    shared["dw"] = np.ascontiguousarray(np.asarray(inputs["L1_ffn_down_w"], dtype=np.int32))
    shared["gsc"] = np.ascontiguousarray(np.asarray(inputs["L1_ffn_gate_scale"], dtype=np.float32))
    shared["usc"] = np.ascontiguousarray(np.asarray(inputs["L1_ffn_up_scale"], dtype=np.float32))
    shared["dsc"] = np.ascontiguousarray(np.asarray(inputs["L1_ffn_down_scale"], dtype=np.float32))
    shared.update(host_consts())
    per_core = []
    for b in range(NCORES):
        d = dict(shared)
        d["x0fm"] = np.ascontiguousarray(x[b].T.astype(np.float32))
        per_core.append(d)
    return per_core


INPUT_SPECS = {
    "x0fm": ((D, T), F32),
    "l0wq": ((D, D), F32), "l0wk": ((D, D), F32), "l0wv": ((D, D), F32),
    "l0bq": ((D,), F32), "l0bk": ((D,), F32), "l0bv": ((D,), F32),
    "l1wq": ((D, D), F32), "l1wk": ((D, D), F32), "l1wv": ((D, D), F32),
    "l1bq": ((D,), F32), "l1bk": ((D,), F32), "l1bv": ((D,), F32),
    "sel_wall": ((D, 64), F32), "sel_muT": ((D, 8), F32), "sel_mu": ((8, D), F32),
    "sel_muexp": ((D, 64), F32), "sel_uall": ((D, 64), F32), "sel_dbexp": ((D, 64), F32),
    "sel_dg": ((64, D), F32), "sel_bias": ((8, D), F32),
    "imp_c": ((8,), F32), "imp_s": ((8,), F32),
    "gw": ((FF, D), I32), "uw": ((FF, D), I32), "dw": ((D, FF), I32),
    "gsc": ((FF,), F32), "usc": ((FF,), F32), "dsc": ((D,), F32),
    "ident_bf": ((128, 128), BF16), "ident_f32": ((128, 128), F32),
    "ones_row_bf": ((1, TT), BF16), "ones128_row_bf": ((1, 128), BF16),
    "ones_col_bf": ((128, 1), BF16), "ones8_row_bf": ((1, 8), BF16),
    "G8_bf": ((64, 8), BF16), "E8_bf": ((8, 64), BF16),
    "C16": ((128, 8), F32), "cc": ((1, 4), F32),
    "dt_bf": ((RH, CH, CH), BF16), "cs_f32": ((RH, CH), BF16), "kv_f32": ((RH, CH), BF16),
    "iota64_bc": ((128, 64), F32), "iota128_bc": ((64, 128), F32),
    "bos_eos": ((128, 32), F32), "beosT": ((32, 128), F32), "ustrict": ((32, 32), BF16),
    "ones64_row": ((1, 64), BF16), "iota64_col": ((64, 1), F32),
}

SCRATCH = {
    "xn0": ((D, T), BF16),
    "p0": ((D, T), F32),
    "q0s": ((D, T), BF16),
    "k0s": ((D, T), BF16),
    "x1": ((D, T), F32),
    "xn1": ((D, T), BF16),
    "q1s": ((D, T), BF16),
    "q1t": ((LH, T, LD), BF16),
    "k1s": ((D, T), BF16),
    "v1t": ((LH, T, LD), BF16),
    "nrm": ((LH, T), F32),
    "msk": ((LH, T), F32),
    "ap1": ((D, T), F32),
    "wgb": ((FF, D), BF16),
    "wub": ((FF, D), BF16),
    "wdb": ((D, FF), BF16),
    "ha": ((FF, T), BF16),
    "outfm": ((D, T), F32),
    "rat": ((LH, 32, 128), BF16),
    "ra": ((LH, 128, 32), F32),
}


class Prog:
    def __init__(self, nc):
        self.nc = nc
        self.q = {e: [] for e in ENGINES}
        self.sems = {}
        self.cnt = {}

    def sem(self, name):
        if name not in self.sems:
            self.sems[name] = self.nc.alloc_semaphore(name)
            self.cnt[name] = 0
        return self.sems[name]

    def wait(self, eng, name, val=None):
        h = self.sem(name)
        v = self.cnt[name] if val is None else val
        if v > 0:
            self.q[eng].append(lambda e, h=h, v=v: e.wait_ge(h, v))

    def mark(self, name):
        self.sem(name)
        return self.cnt[name]

    def run(self, eng, fn, inc=None, inc_n=1):
        if inc is not None:
            h = self.sem(inc)
            self.cnt[inc] += inc_n
            self.q[eng].append(lambda e, fn=fn, h=h, n=inc_n: fn(e).then_inc(h, n))
        else:
            self.q[eng].append(lambda e, fn=fn: fn(e))

    def barrier(self, name, engines=("scalar", "vector", "tensor", "gpsimd", "sync")):
        h = self.sem(name)
        n = len(engines)
        base = self.cnt[name]
        for e in engines:
            self.cnt[name] += 1
            self.q[e].append(lambda eng, h=h: eng.sem_inc(h, 1))
        tgt = base + n
        for e in engines:
            self.q[e].append(lambda eng, h=h, v=tgt: eng.wait_ge(h, v))

    def emit(self):
        with self.nc.Block() as block:
            for eng in ENGINES:
                if not self.q[eng]:
                    continue

                def make(qlist):
                    def body(e):
                        for fn in qlist:
                            fn(e)
                    return body
                getattr(block, eng)(make(self.q[eng]))


def _dram(nc, name, shape, dtype, kind):
    return nc.dram_tensor(name, list(shape), dtype, kind=kind).ap()


def _sb(nc, name, shape, dtype):
    return nc.alloc_sbuf_tensor(name, list(shape), dtype).ap()


def _ps(nc, name, shape=(128, TT), dtype=F32):
    return nc.alloc_psum_tensor(name, list(shape), dtype).ap()


def _dma(e, dst, src):
    return e.dma_start(out=dst, in_=src)


DEBUG0A = False


def emit_phase0a(p, io, sb, PS):
    nc = p.nc
    C = sb

    xf = [_sb(nc, "a_xf0", (128, FC * TT), F32)] * 2
    xnb = [_sb(nc, f"a_xnb{i}", (128, FC * TT), BF16) for i in range(2)]
    ubf = _sb(nc, "a_ubf", (128, FC * TT), BF16)
    xcr = _sb(nc, "a_xcr", (128, 2 * TT), BF16)
    xsr = _sb(nc, "a_xsr", (128, 2 * TT), BF16)
    tscr = _sb(nc, "a_tscr", (128, 2 * TT), F32)
    wsc = _sb(nc, "a_wsc", (128, 2 * TT), F32)
    xpv = _sb(nc, "a_xpv", (128, 2 * TT), F32)
    p0st = _sb(nc, "a_p0st", (128, 2 * TT), F32)
    bBs = _sb(nc, "a_bBs", (128, TT), F32)
    selBs = _sb(nc, "a_selBs", (128, TT), F32)
    Em = _sb(nc, "a_Em", (1, TT), F32)
    Em2 = _sb(nc, "a_Em2", (1, TT), F32)
    var = _sb(nc, "a_var", (1, TT), F32)
    rstd = _sb(nc, "a_rstd", (1, TT), F32)
    brow = _sb(nc, "a_brow", (1, TT), F32)
    rs2 = _sb(nc, "a_rs2", (1, TT), F32)
    nrm2 = _sb(nc, "a_nrm2", (1, TT), F32)
    lor = _sb(nc, "a_lor", (1, TT), F32)
    ahi = _sb(nc, "a_ahi", (1, TT), BF16)
    alo = _sb(nc, "a_alo", (1, TT), BF16)
    bhi = _sb(nc, "a_bhi", (1, TT), BF16)
    blo = _sb(nc, "a_blo", (1, TT), BF16)
    n2hi = _sb(nc, "a_n2hi", (1, TT), BF16)
    n2lo = _sb(nc, "a_n2lo", (1, TT), BF16)
    selhi = _sb(nc, "a_selhi", (1, TT), BF16)
    sello = _sb(nc, "a_sello", (1, TT), BF16)
    selv = _sb(nc, "a_selv", (1, TT), F32)
    sellov = _sb(nc, "a_sellov", (1, TT), F32)
    wallf = _sb(nc, "a_wallf", (128, FC * 64), F32)
    wallb = _sb(nc, "a_wallb", (128, FC * 64), BF16)
    muTf = _sb(nc, "a_muTf", (128, FC * 8), F32)
    muSb = _sb(nc, "a_muSb", (128, FC * 8), BF16)
    uallf = _sb(nc, "a_uallf", (128, FC * 64), F32)
    uallb = _sb(nc, "a_uallb", (128, FC * 64), BF16)
    scrf = _sb(nc, "a_scrf", (128, FC * 64), F32)
    prodb = _sb(nc, "a_prodb", (128, FC * 64), BF16)
    stage = _sb(nc, "a_stage", (64, D), F32)
    dgb = _sb(nc, "a_dgb", (64, D), BF16)
    biasb = _sb(nc, "a_biasb", (8, D), BF16)
    musq = _sb(nc, "a_musq", (8, D), F32)
    mu2c = _sb(nc, "a_mu2c", (8, 1), F32)
    cwrow = _sb(nc, "a_cwrow", (1, 128), BF16)
    ic_col = _sb(nc, "a_iccol", (8, 1), F32)
    is_col = _sb(nc, "a_iscol", (8, 1), F32)
    c16t = _sb(nc, "a_c16", (128, 8), F32)
    ysqb = _sb(nc, "a_ysqb", (64, TT), BF16)
    m64b = _sb(nc, "a_m64b", (64, TT), BF16)
    modxb = _sb(nc, "a_modxb", (64, TT), BF16)
    mbf = _sb(nc, "a_mbf", (8, TT), BF16)
    r2full = _sb(nc, "a_r2full", (8, TT), F32)
    rec = _sb(nc, "a_rec", (8, TT), F32)
    score = _sb(nc, "a_score", (8, TT), F32)
    zrow = _sb(nc, "a_zrow", (8, TT), F32)
    gate8 = _sb(nc, "a_gate8", (8, TT), F32)
    gtm = _sb(nc, "a_gtm", (128, 32), F32)
    penm = _sb(nc, "a_penm", (128, 32), F32)
    seltm = _sb(nc, "a_seltm", (128, 4), F32)
    firstc = _sb(nc, "a_first", (128, 4), F32)

    B0, B1, B2, B3, B4, B5, B6, B7 = PS

    p.run("sync", lambda e: _dma(e, wallf[:].rearrange("p (c s) -> p c s", c=16), io["sel_wall"].rearrange("(c p) s -> p c s", p=128)), "L", 16)
    p.run("sync", lambda e: _dma(e, muTf[:].rearrange("p (c s) -> p c s", c=16), io["sel_muT"].rearrange("(c p) s -> p c s", p=128)), "L", 16)
    p.run("sync", lambda e: _dma(e, uallf[:].rearrange("p (c s) -> p c s", c=16), io["sel_uall"].rearrange("(c p) s -> p c s", p=128)), "L", 16)
    p.run("sync", lambda e: _dma(e, stage[:], io["sel_dg"]), "L", 16)
    p.run("sync", lambda e: _dma(e, ic_col[:, 0:1], io["imp_c"].rearrange("(p o) -> p o", o=1)), "L", 16)
    p.run("sync", lambda e: _dma(e, is_col[:, 0:1], io["imp_s"].rearrange("(p o) -> p o", o=1)), "L", 16)
    p.run("sync", lambda e: _dma(e, c16t[:], io["C16"]), "L", 16)
    p.run("scalar", lambda e: _dma(e, scrf[:].rearrange("p (c s) -> p c s", c=16), io["sel_muexp"].rearrange("(c p) s -> p c s", p=128)), "L2", 16)
    ld1 = p.mark("L")
    ld2 = p.mark("L2")

    p.wait("scalar", "L", ld1)
    p.run("scalar", lambda e: e.activation(out=wallb[:], in_=wallf[:], func=AF.Copy), "P", 1)
    p.run("scalar", lambda e: e.activation(out=uallb[:], in_=uallf[:], func=AF.Copy), "P", 1)
    p.run("scalar", lambda e: e.activation(out=dgb[:], in_=stage[:], func=AF.Copy), "DGC", 1)
    p.wait("sync", "DGC")
    p.run("sync", lambda e: _dma(e, stage[0:8, :], io["sel_bias"]), "L3", 16)
    p.wait("scalar", "L3")
    p.run("scalar", lambda e: e.activation(out=biasb[:], in_=stage[0:8, :], func=AF.Copy), "BIC", 1)
    p.wait("sync", "BIC")
    p.run("sync", lambda e: _dma(e, stage[0:8, :], io["sel_mu"]), "L4", 16)
    p.wait("scalar", "L4")
    p.run("scalar", lambda e: e.activation(out=musq[:], in_=stage[0:8, :], func=AF.Square,
                                           accum_out=mu2c[:, 0:1]), "P", 1)
    p.wait("vector", "L", ld1)
    p.run("vector", lambda e: e.tensor_scalar(out=muSb[:], in0=muTf[:], scalar1=-2.0, scalar2=None,
                                              op0=ALU.mult), "P", 1)
    p.wait("vector", "L2", ld2)
    p.run("vector", lambda e: e.tensor_tensor(out=prodb[:], in0=wallf[:], in1=scrf[:], op=ALU.mult),
          "PR1", 1)
    p.run("scalar", lambda e: _dma(e, scrf[:].rearrange("p (c s) -> p c s", c=16), io["sel_dbexp"].rearrange("(c p) s -> p c s", p=128)),
          "L2", 16)
    ld2b = p.mark("L2")
    p.wait("tensor", "PR1")
    for kc in range(FC):
        p.run("tensor", lambda e, kc=kc: e.matmul(B3[0:1, 0:64], C["ones_col"][:],
                                                  prodb[:, kc * 64:(kc + 1) * 64],
                                                  start=(kc == 0), stop=(kc == FC - 1)),
              "MMc" if kc == FC - 1 else None)
    p.wait("vector", "L2", ld2b)
    p.wait("vector", "MMc")
    p.run("vector", lambda e: e.tensor_tensor(out=prodb[:], in0=uallf[:], in1=scrf[:], op=ALU.mult),
          "PR2", 1)
    p.wait("tensor", "PR2")
    for kc in range(FC):
        p.run("tensor", lambda e, kc=kc: e.matmul(B3[0:1, 64:128], C["ones_col"][:],
                                                  prodb[:, kc * 64:(kc + 1) * 64],
                                                  start=(kc == 0), stop=(kc == FC - 1)),
              "MMc" if kc == FC - 1 else None)
    p.wait("scalar", "MMc")
    p.run("scalar", lambda e: e.activation(out=cwrow[:], in_=B3[0:1, 0:128],
                                           func=AF.Identity, scale=-1.0), "CW", 1)
    cw_mark = p.mark("CW")

    xf_done = []
    uc_marks = []
    xn_done = []
    xns_marks = []
    mks_marks = []
    mbf_marks = []
    for t in range(NTT):
        bi = t % 2
        x_t = xf[bi]
        xn_t = xnb[bi]
        c0 = t * TT

        if t >= 1:
            p.wait("sync", "UC", uc_marks[t - 1])
        for fcx in range(FC):
            p.run("sync", lambda e, fcx=fcx, x_t=x_t, c0=c0:
                  _dma(e, x_t[:, fcx * TT:(fcx + 1) * TT],
                       io["x0fm"][fcx * 128:(fcx + 1) * 128, c0:c0 + TT]), "XT", 16)
        xt_mark = p.mark("XT")

        p.wait("scalar", "XT", xt_mark)
        p.wait("tensor", "XT", xt_mark)
        if t >= 1:
            p.wait("tensor", "P0V", xn_done[t - 1])
            p.wait("tensor", "MKS", mks_marks[t - 1])
            p.wait("tensor", "MBF", mbf_marks[t - 1])
        smm_base = p.mark("SMM")
        for kc in range(FC):
            r = kc % 2
            p.wait("scalar", "SMM", smm_base + max(0, kc - 1))
            p.run("scalar", lambda e, kc=kc, r=r, x_t=x_t: e.activation(
                out=xsr[:, r * TT:(r + 1) * TT], in_=x_t[:, kc * TT:(kc + 1) * TT],
                func=AF.Square))
            p.run("scalar", lambda e, kc=kc, r=r, x_t=x_t: e.activation(
                out=xcr[:, r * TT:(r + 1) * TT], in_=x_t[:, kc * TT:(kc + 1) * TT],
                func=AF.Copy), "SC", 1)
            p.wait("tensor", "SC")
            p.run("tensor", lambda e, kc=kc, r=r: e.matmul(
                B0[0:1, :], C["ones_col"][:], xcr[:, r * TT:(r + 1) * TT],
                start=(kc == 0), stop=(kc == FC - 1)))
            p.run("tensor", lambda e, kc=kc, r=r: e.matmul(
                B0[32:33, :], C["ones_col"][:], xsr[:, r * TT:(r + 1) * TT],
                start=(kc == 0), stop=(kc == FC - 1)),
                "SMM", 1)
        smm_mark = p.mark("SMM")

        p.wait("vector", "SMM", smm_mark)
        p.run("vector", lambda e: e.tensor_scalar(out=Em, in0=B0[0:1, :], scalar1=1.0 / D,
                                                  scalar2=None, op0=ALU.mult))
        p.run("vector", lambda e: e.tensor_tensor(out=Em2, in0=Em, in1=Em, op=ALU.mult))
        p.run("vector", lambda e: e.scalar_tensor_tensor(out=var, in0=B0[32:33, :], scalar=1.0 / D,
                                                         in1=Em2, op0=ALU.mult, op1=ALU.subtract),
              "VAR", 1)
        p.wait("scalar", "VAR")
        p.run("scalar", lambda e: e.activation(out=rstd, in_=var, func=AF.Sqrt,
                                               bias=C["cc"][0:1, 0:1], scale=1.0), "RSD", 1)
        p.wait("vector", "RSD")
        p.run("vector", lambda e: e.reciprocal(out=rstd, in_=rstd))
        p.run("vector", lambda e: e.scalar_tensor_tensor(out=brow, in0=Em, scalar=-1.0, in1=rstd,
                                                         op0=ALU.mult, op1=ALU.mult))
        p.run("vector", lambda e: e.tensor_tensor(out=rs2, in0=rstd, in1=rstd, op=ALU.mult))
        p.run("vector", lambda e: e.scalar_tensor_tensor(out=nrm2, in0=var, scalar=float(D),
                                                         in1=rs2, op0=ALU.mult, op1=ALU.mult))
        for hh, ll, ss in ((ahi, alo, rstd), (bhi, blo, brow), (n2hi, n2lo, nrm2)):
            p.run("vector", lambda e, hh=hh, ss=ss: e.tensor_copy(out=hh[:], in_=ss))
            p.run("vector", lambda e, hh=hh, ss=ss, lor=lor: e.tensor_tensor(out=lor, in0=ss, in1=hh,
                                                                             op=ALU.subtract))
            last = (ss is nrm2)
            p.run("vector", lambda e, ll=ll, lor=lor: e.tensor_copy(out=ll[:], in_=lor),
                  "HL" if last else None, 1)
        p.wait("tensor", "HL")
        p.run("tensor", lambda e: e.matmul(B1[:, :], C["o128"][:], ahi[:], start=True, stop=False))
        p.run("tensor", lambda e: e.matmul(B1[:, :], C["o128"][:], alo[:], start=False, stop=True), "AB", 1)
        p.run("tensor", lambda e: e.matmul(B2[:, :], C["o128"][:], bhi[:], start=True, stop=False))
        p.run("tensor", lambda e: e.matmul(B2[:, :], C["o128"][:], blo[:], start=False, stop=True), "BB", 1)
        p.wait("vector", "BB")
        p.wait("vector", "XNC")
        p.run("vector", lambda e: e.tensor_copy(out=bBs[:], in_=B2[:, :]), "BBS", 1)
        bbs_mark = p.mark("BBS")
        if t >= 2:
            p.wait("gpsimd", "P0V", xn_done[t - 2])
            p.wait("gpsimd", "XNS", xns_marks[t - 2])
        xnc_base = p.mark("XNC")
        for kc in range(FC):
            r = kc % 2
            if kc >= 2:
                p.wait("vector", "XNC", xnc_base + kc - 1)
            p.run("vector", lambda e, kc=kc, r=r, x_t=x_t: e.tensor_tensor(
                out=tscr[:, r * TT:(r + 1) * TT], in0=x_t[:, kc * TT:(kc + 1) * TT],
                in1=B1[:, :], op=ALU.mult), "TSC", 1)
            p.wait("gpsimd", "TSC")
            if kc == 0:
                p.wait("gpsimd", "BBS", bbs_mark)
            p.run("gpsimd", lambda e, kc=kc, r=r, xn_t=xn_t: e.tensor_tensor(
                out=xn_t[:, kc * TT:(kc + 1) * TT], in0=tscr[:, r * TT:(r + 1) * TT],
                in1=bBs[:], op=ALU.add), "XNC", 1)
            p.run("gpsimd", lambda e, kc=kc, r=r, xn_t=xn_t, x_t=x_t: e.tensor_tensor(
                out=ubf[:, kc * TT:(kc + 1) * TT], in0=x_t[:, kc * TT:(kc + 1) * TT],
                in1=xn_t[:, kc * TT:(kc + 1) * TT], op=ALU.add), "UC", 1)
        xn_mark = p.mark("XNC")
        p.wait("scalar", "XNC", xn_mark)
        for fcx in range(FC):
            p.run("scalar", lambda e, fcx=fcx, xn_t=xn_t, c0=c0:
                  _dma(e, io["xn0"][fcx * 128:(fcx + 1) * 128, c0:c0 + TT],
                       xn_t[:, fcx * TT:(fcx + 1) * TT]), "XNS", 16)
        p.wait("tensor", "XNC", xn_mark)
        if t == 0:
            p.wait("tensor", "CW", cw_mark)
            p.wait("tensor", "P")
        for kc in range(FC):
            p.run("tensor", lambda e, kc=kc, xn_t=xn_t: e.matmul(
                B3[0:64, :], wallb[:, kc * 64:(kc + 1) * 64],
                xn_t[:, kc * TT:(kc + 1) * TT], start=(kc == 0), stop=False))
        p.run("tensor", lambda e: e.matmul(B3[0:64, :], cwrow[0:1, 0:64], C["ones_r"][:],
                                           start=False, stop=True), "Y", 1)
        for kc in range(FC):
            p.run("tensor", lambda e, kc=kc, xn_t=xn_t: e.matmul(
                B4[0:8, :], muSb[:, kc * 8:(kc + 1) * 8],
                xn_t[:, kc * TT:(kc + 1) * TT], start=(kc == 0), stop=False))
        p.run("tensor", lambda e: e.matmul(B4[0:8, :], C["ones8_r"][:], n2hi[:], start=False, stop=False))
        p.run("tensor", lambda e: e.matmul(B4[0:8, :], C["ones8_r"][:], n2lo[:], start=False, stop=True),
              "R2", 1)
        for kc in range(FC):
            p.run("tensor", lambda e, kc=kc, xn_t=xn_t: e.matmul(
                B5[0:64, :], uallb[:, kc * 64:(kc + 1) * 64],
                xn_t[:, kc * TT:(kc + 1) * TT], start=(kc == 0), stop=False))
        p.run("tensor", lambda e: e.matmul(B5[0:64, :], cwrow[0:1, 64:128], C["ones_r"][:],
                                           start=False, stop=True), "XU", 1)
        p.wait("scalar", "Y")
        p.run("scalar", lambda e: e.activation(out=ysqb[:], in_=B3[0:64, :], func=AF.Square), "YS", 1)
        p.wait("tensor", "YS")
        p.run("tensor", lambda e: e.matmul(B3[64:72, :], C["g8"][:], ysqb[:], start=True, stop=True),
              "SCM", 1)
        p.wait("vector", "R2")
        p.run("vector", lambda e: e.tensor_scalar(out=r2full, in0=B4[0:8, :], scalar1=mu2c[:, 0:1],
                                                  scalar2=None, op0=ALU.add), "R2C", 1)
        p.run("vector", lambda e: e.reciprocal(out=rec, in_=r2full))
        p.wait("vector", "SCM")
        p.run("vector", lambda e: e.tensor_tensor(out=score, in0=B3[64:72, :], in1=rec, op=ALU.mult))
        p.run("vector", lambda e: e.tensor_scalar(out=zrow, in0=score, scalar1=ic_col[:, 0:1],
                                                  scalar2=is_col[:, 0:1], op0=ALU.subtract,
                                                  op1=ALU.mult), "Z", 1)
        p.wait("scalar", "Z")
        p.run("scalar", lambda e: e.activation(out=gate8[:], in_=zrow, func=AF.Sigmoid), "G", 1)
        p.wait("tensor", "G")
        p.wait("tensor", "BBS")
        for s in range(4):
            p.run("tensor", lambda e, s=s: e.transpose(B2[0:128, s * 8:(s + 1) * 8],
                                                       gate8[:, s * 128:(s + 1) * 128],
                                                       C["idf"][0:8, 0:8]),
                  "GT" if s == 3 else None)
        p.wait("vector", "GT")
        p.run("vector", lambda e: e.tensor_copy(out=gtm[:], in_=B2[0:128, 0:32]), "GTM", 1)
        for s in range(4):
            gs = gtm[:, s * 8:(s + 1) * 8]
            pen = penm[:, s * 8:(s + 1) * 8]
            mx = seltm[:, s:s + 1]
            fc_ = firstc[:, s:s + 1]
            p.run("vector", lambda e, gs=gs, mx=mx: e.reduce_max(out=mx, in_=gs, axis=AX.X),
                  "VSQ", 1)
            p.wait("vector", "VSQ")
            p.run("vector", lambda e, gs=gs, mx=mx, pen=pen: e.tensor_scalar(
                out=pen, in0=gs, scalar1=mx, scalar2=None, op0=ALU.is_ge), "VSQ", 1)
            p.wait("vector", "VSQ")
            p.run("vector", lambda e, pen=pen: e.scalar_tensor_tensor(
                out=pen, in0=pen, scalar=-16.0, in1=c16t[:], op0=ALU.mult, op1=ALU.add), "VSQ", 1)
            p.wait("vector", "VSQ")
            p.run("vector", lambda e, pen=pen, fc_=fc_: e.tensor_reduce(
                out=fc_, in_=pen, axis=AX.X, op=ALU.min), "VSQ", 1)
            p.wait("vector", "VSQ")
            p.run("vector", lambda e, pen=pen, fc_=fc_, gs=gs: e.tensor_scalar(
                out=gs, in0=pen, scalar1=fc_, scalar2=None, op0=ALU.is_le),
                "MTM" if s == 3 else None, 1)
        p.wait("tensor", "MTM")
        p.wait("tensor", "VAR")
        for s in range(4):
            p.run("tensor", lambda e, s=s: e.transpose(B0[0:8, s * 128:(s + 1) * 128],
                                                       gtm[:, s * 8:(s + 1) * 8], C["idf"][:]),
                  "MT" if s == 3 else None)
        p.wait("tensor", "GTM")
        for s in range(4):
            p.run("tensor", lambda e, s=s: e.transpose(B2[0:1, s * 128:(s + 1) * 128],
                                                       seltm[:, s:s + 1], C["idf"][:]),
                  "ST" if s == 3 else None)
        p.wait("scalar", "MT")
        p.run("scalar", lambda e: e.activation(out=mbf[:], in_=B0[0:8, 0:TT], func=AF.Copy), "MBF", 1)
        p.wait("vector", "ST")
        p.run("vector", lambda e: e.tensor_copy(out=selv[:], in_=B2[0:1, 0:TT]))
        p.run("vector", lambda e: e.tensor_copy(out=selhi[:], in_=selv[:]))
        p.run("vector", lambda e: e.tensor_tensor(out=sellov[:], in0=selv[:],
                                                  in1=selhi[:], op=ALU.subtract))
        p.run("vector", lambda e: e.tensor_copy(out=sello[:], in_=sellov[:]), "SBF", 1)
        p.wait("tensor", "SBF")
        p.wait("tensor", "TSC")
        p.run("tensor", lambda e: e.matmul(B1[:, :], C["o128"][:], selhi[:],
                                           start=True, stop=False))
        p.run("tensor", lambda e: e.matmul(B1[:, :], C["o128"][:], sello[:],
                                           start=False, stop=True), "SELB", 1)
        p.wait("vector", "SELB")
        p.run("vector", lambda e: e.tensor_copy(out=selBs[:], in_=B1[:, :]), "SBS", 1)
        p.wait("tensor", "MBF")
        p.wait("tensor", "R2C")
        p.run("tensor", lambda e: e.matmul(B4[0:64, :], C["e8"][:], mbf[:], start=True, stop=True),
              "MK", 1)
        p.wait("scalar", "MK")
        p.run("scalar", lambda e: e.activation(out=m64b[:], in_=B4[0:64, :], func=AF.Copy), "MKS", 1)
        p.wait("vector", "MKS")
        p.wait("vector", "XU")
        p.run("vector", lambda e: e.tensor_tensor(out=modxb[:], in0=B5[0:64, :], in1=m64b[:],
                                                  op=ALU.mult), "MX", 1)
        p.wait("tensor", "MX")
        p.wait("tensor", "SBS")
        sbs_mark = p.mark("SBS")
        p0c_base = p.mark("P0C")
        for jc in range(FC):
            bank = B6 if (jc % 2 == 0) else B7
            sem = f"FF{jc % 2}"
            if jc >= 2:
                p.wait("tensor", "P0V", p.cnt.get("P0V", 0) - 1)
            p.run("tensor", lambda e, jc=jc, bank=bank: e.matmul(
                bank[:, :], dgb[:, jc * 128:(jc + 1) * 128], modxb[:], start=True, stop=False))
            p.run("tensor", lambda e, jc=jc, bank=bank: e.matmul(
                bank[:, :], biasb[:, jc * 128:(jc + 1) * 128], mbf[:], start=False, stop=True),
                sem, 1)
            r = jc % 2
            p.wait("vector", sem)
            if jc == 0:
                p.wait("vector", "SBS", sbs_mark)
            if jc >= 2:
                p.wait("vector", "P0C", p0c_base + jc - 1)
            else:
                p.wait("vector", "P0C", p0c_base)
            p.run("vector", lambda e, jc=jc, bank=bank, r=r, xn_t=xn_t: e.tensor_tensor(
                out=wsc[:, r * TT:(r + 1) * TT], in0=bank[:, :],
                in1=xn_t[:, jc * TT:(jc + 1) * TT], op=ALU.subtract))
            p.run("vector", lambda e, r=r: e.tensor_tensor(
                out=wsc[:, r * TT:(r + 1) * TT], in0=wsc[:, r * TT:(r + 1) * TT],
                in1=selBs[:], op=ALU.mult), "P0V", 1)
            p.wait("gpsimd", "P0V")
            if jc >= 2:
                p.wait("gpsimd", "P0S", p.cnt.get("P0S", 0) - 16)
            else:
                p.wait("gpsimd", "P0S", p.cnt.get("P0S", 0))
            p.run("gpsimd", lambda e, jc=jc, r=r: e.tensor_tensor(
                out=p0st[:, r * TT:(r + 1) * TT], in0=wsc[:, r * TT:(r + 1) * TT],
                in1=ubf[:, jc * TT:(jc + 1) * TT], op=ALU.add), "P0C", 1)
            p.wait("sync", "P0C")
            p.run("sync", lambda e, jc=jc, r=r, c0=c0:
                  _dma(e, io["p0"][jc * 128:(jc + 1) * 128, c0:c0 + TT],
                       p0st[:, r * TT:(r + 1) * TT]), "P0S", 16)
        if DEBUG0A and t == 0:
            p.wait("gpsimd", "MX")
            p.run("gpsimd", lambda e: _dma(e, io["dbg3"][0:64, :], m64b[:]), "DBG", 16)
            p.run("gpsimd", lambda e: _dma(e, io["dbg3"][64:128, :], modxb[:]), "DBG", 16)
            p.wait("gpsimd", "MTM")
            p.run("gpsimd", lambda e: _dma(e, io["dbg2"][:, 0:32], gtm[:]), "DBG", 16)
            p.run("gpsimd", lambda e: _dma(e, io["dbg2"][:, 32:64], penm[:]), "DBG", 16)
            p.run("gpsimd", lambda e: _dma(e, io["dbg2"][:, 64:68], firstc[:]), "DBG", 16)
            p.run("gpsimd", lambda e: _dma(e, io["dbg2"][:, 68:72], seltm[:]), "DBG", 16)
        if DEBUG0A:
            p.wait("gpsimd", "G")
            p.run("gpsimd", lambda e, c0=c0: _dma(e, io["dbg"][0:8, c0:c0 + TT], gate8[:]), "DBG", 16)
            p.wait("gpsimd", "MBF")
            p.run("gpsimd", lambda e, c0=c0: _dma(e, io["dbg"][8:16, c0:c0 + TT], mbf[:]), "DBG", 16)
            p.wait("gpsimd", "SBF")
            p.run("gpsimd", lambda e, c0=c0: _dma(e, io["dbg"][16:17, c0:c0 + TT], selv[:]), "DBG", 16)
            p.wait("gpsimd", "Z")
            p.run("gpsimd", lambda e, c0=c0: _dma(e, io["dbg"][17:25, c0:c0 + TT], score[:]), "DBG", 16)
            p.wait("gpsimd", "R2C")
            p.run("gpsimd", lambda e, c0=c0: _dma(e, io["dbg"][25:33, c0:c0 + TT], r2full[:]), "DBG", 16)
        xf_done.append(p.mark("P0C"))
        uc_marks.append(p.mark("UC"))
        xn_done.append(p.mark("P0V"))
        xns_marks.append(p.mark("XNS"))
        mks_marks.append(p.mark("MKS"))
        mbf_marks.append(p.mark("MBF"))

    p.wait("gpsimd", "P0S")
    p.wait("gpsimd", "XNS")




def emit_ln(p, io, C, PS, src_name, dst_name):
    nc = p.nc
    xf = _sb(nc, "n_xf", (128, FC * TT), F32)
    xcr = _sb(nc, "n_xcr", (128, 2 * TT), BF16)
    xsr = _sb(nc, "n_xsr", (128, 2 * TT), BF16)
    tscr = _sb(nc, "n_tscr", (128, 2 * TT), F32)
    xnb = _sb(nc, "n_xnb", (128, FC * TT), BF16)
    bBs = _sb(nc, "n_bBs", (128, TT), F32)
    Em = _sb(nc, "n_Em", (1, TT), F32)
    Em2 = _sb(nc, "n_Em2", (1, TT), F32)
    var = _sb(nc, "n_var", (1, TT), F32)
    rstd = _sb(nc, "n_rstd", (1, TT), F32)
    brow = _sb(nc, "n_brow", (1, TT), F32)
    lor = _sb(nc, "n_lor", (1, TT), F32)
    ahi = _sb(nc, "n_ahi", (1, TT), BF16)
    alo = _sb(nc, "n_alo", (1, TT), BF16)
    bhi = _sb(nc, "n_bhi", (1, TT), BF16)
    blo = _sb(nc, "n_blo", (1, TT), BF16)
    B0, B1, B2 = PS[0], PS[1], PS[2]

    for t in range(NTT):
        c0 = t * TT
        if t >= 1:
            p.wait("sync", "nTS", p.cnt.get("nTS", 0))
        for fcx in range(FC):
            p.run("sync", lambda e, fcx=fcx, c0=c0: _dma(
                e, xf[:, fcx * TT:(fcx + 1) * TT],
                io[src_name][fcx * 128:(fcx + 1) * 128, c0:c0 + TT]), "nXT", 16)
        xt_mark = p.mark("nXT")
        p.wait("scalar", "nXT", xt_mark)
        if t >= 1:
            p.wait("tensor", "nVAR", p.cnt.get("nVAR", 0))
        smm_base = p.mark("nSM")
        for kc in range(FC):
            r = kc % 2
            p.wait("scalar", "nSM", smm_base + max(0, kc - 1))
            p.run("scalar", lambda e, kc=kc, r=r: e.activation(
                out=xsr[:, r * TT:(r + 1) * TT], in_=xf[:, kc * TT:(kc + 1) * TT], func=AF.Square))
            p.run("scalar", lambda e, kc=kc, r=r: e.activation(
                out=xcr[:, r * TT:(r + 1) * TT], in_=xf[:, kc * TT:(kc + 1) * TT], func=AF.Copy),
                "nSC", 1)
            p.wait("tensor", "nSC")
            p.run("tensor", lambda e, kc=kc, r=r: e.matmul(
                B0[0:1, :], C["ones_col"][:], xcr[:, r * TT:(r + 1) * TT],
                start=(kc == 0), stop=(kc == FC - 1)))
            p.run("tensor", lambda e, kc=kc, r=r: e.matmul(
                B0[32:33, :], C["ones_col"][:], xsr[:, r * TT:(r + 1) * TT],
                start=(kc == 0), stop=(kc == FC - 1)), "nSM", 1)
        sm = p.mark("nSM")
        p.wait("vector", "nSM", sm)
        p.run("vector", lambda e: e.tensor_scalar(out=Em[:], in0=B0[0:1, :], scalar1=1.0 / D,
                                                  scalar2=None, op0=ALU.mult))
        p.run("vector", lambda e: e.tensor_tensor(out=Em2[:], in0=Em[:], in1=Em[:], op=ALU.mult))
        p.run("vector", lambda e: e.scalar_tensor_tensor(out=var[:], in0=B0[32:33, :], scalar=1.0 / D,
                                                         in1=Em2[:], op0=ALU.mult, op1=ALU.subtract),
              "nVAR", 1)
        p.wait("scalar", "nVAR")
        p.run("scalar", lambda e: e.activation(out=rstd[:], in_=var[:], func=AF.Sqrt,
                                               bias=C["cc"][0:1, 0:1], scale=1.0), "nRS", 1)
        p.wait("vector", "nRS")
        p.run("vector", lambda e: e.reciprocal(out=rstd[:], in_=rstd[:]))
        p.run("vector", lambda e: e.scalar_tensor_tensor(out=brow[:], in0=Em[:], scalar=-1.0,
                                                         in1=rstd[:], op0=ALU.mult, op1=ALU.mult))
        p.run("vector", lambda e: e.tensor_copy(out=ahi[:], in_=rstd[:]))
        p.run("vector", lambda e: e.tensor_tensor(out=lor[:], in0=rstd[:], in1=ahi[:], op=ALU.subtract))
        p.run("vector", lambda e: e.tensor_copy(out=alo[:], in_=lor[:]))
        p.run("vector", lambda e: e.tensor_copy(out=bhi[:], in_=brow[:]))
        p.run("vector", lambda e: e.tensor_tensor(out=lor[:], in0=brow[:], in1=bhi[:], op=ALU.subtract))
        p.run("vector", lambda e: e.tensor_copy(out=blo[:], in_=lor[:]), "nHL", 1)
        p.wait("tensor", "nHL")
        if t >= 1:
            p.wait("tensor", "nTS", p.cnt.get("nTS", 0))
            p.wait("tensor", "nBBS", p.cnt.get("nBBS", 0))
        p.run("tensor", lambda e: e.matmul(B1[:, :], C["o128"][:], ahi[:], start=True, stop=False))
        p.run("tensor", lambda e: e.matmul(B1[:, :], C["o128"][:], alo[:], start=False, stop=True),
              "nAB", 1)
        p.run("tensor", lambda e: e.matmul(B2[:, :], C["o128"][:], bhi[:], start=True, stop=False))
        p.run("tensor", lambda e: e.matmul(B2[:, :], C["o128"][:], blo[:], start=False, stop=True),
              "nBB", 1)
        p.wait("vector", "nBB")
        p.wait("vector", "nXNC")
        p.run("vector", lambda e: e.tensor_copy(out=bBs[:], in_=B2[:, :]), "nBBS", 1)
        bbs_mark = p.mark("nBBS")
        p.wait("vector", "nAB")
        xnc_base = p.mark("nXNC")
        ts_base = p.mark("nTS")
        for kc in range(FC):
            r = kc % 2
            if kc >= 2:
                p.wait("vector", "nXNC", xnc_base + kc - 1)
            p.run("vector", lambda e, kc=kc, r=r: e.tensor_tensor(
                out=tscr[:, r * TT:(r + 1) * TT], in0=xf[:, kc * TT:(kc + 1) * TT],
                in1=B1[:, :], op=ALU.mult), "nTS", 1)
            p.wait("gpsimd", "nTS")
            if kc == 0:
                p.wait("gpsimd", "nBBS", bbs_mark)
                p.wait("gpsimd", "nXNS", p.cnt.get("nXNS", 0))
            p.run("gpsimd", lambda e, kc=kc, r=r: e.tensor_tensor(
                out=xnb[:, kc * TT:(kc + 1) * TT], in0=tscr[:, r * TT:(r + 1) * TT],
                in1=bBs[:], op=ALU.add), "nXNC", 1)
        xn_mark = p.mark("nXNC")
        p.wait("sync", "nXNC", xn_mark)
        for fcx in range(FC):
            p.run("sync", lambda e, fcx=fcx, c0=c0: _dma(
                e, io[dst_name][fcx * 128:(fcx + 1) * 128, c0:c0 + TT],
                xnb[:, fcx * TT:(fcx + 1) * TT]), "nXNS", 16)
    p.wait("gpsimd", "nXNS")


def emit_proj(p, io, C, PS, src_name, outs, tag, psbf=None):
    nc = p.nc
    nmat = len(outs)
    wsb = [_sb(nc, f"{tag}_w{i}", (128, FC * D), BF16) for i in range(nmat)]
    bcol = [_sb(nc, f"{tag}_b{i}", (128, FC), F32) for i in range(nmat)]
    wstg = _sb(nc, f"{tag}_wstg", (128, 2 * D), F32)
    xnt = [_sb(nc, f"{tag}_xn{i}", (128, FC * TT), BF16) for i in range(2)]
    est = [_sb(nc, f"{tag}_est{i}", (128, 4 * TT), BF16) for i in range(nmat)]
    any_tm = any(o[5] is not None for o in outs)
    if any_tm:
        tmstg = _sb(nc, f"{tag}_tmstg", (128, 2 * TT), BF16)
        B4b = psbf
    if any(o[6] for o in outs):
        nrm_cols = _sb(nc, f"{tag}_nrm", (128, FC * 32), F32)
        nsq = _sb(nc, f"{tag}_nsq", (128, TT), F32)
    else:
        nrm_cols = None

    LD, CV, XL, MM, EV, ST = (f"{tag}L", f"{tag}C", f"{tag}X", f"{tag}M", f"{tag}E", f"{tag}S")
    TP, TE, TS, NQ = (f"{tag}T", f"{tag}TE", f"{tag}TS", f"{tag}NQ")

    for i, o in enumerate(outs):
        bname = o[2]
        for c in range(FC):
            p.run("sync", lambda e, i=i, c=c, bname=bname:
                  _dma(e, bcol[i][:, c:c + 1],
                       io[bname][c * 128:(c + 1) * 128].rearrange("(p o) -> p o", o=1)), LD, 16)
    for i, o in enumerate(outs):
        wname = o[1]
        for c in range(FC):
            p.run("sync", lambda e, i=i, c=c, wname=wname:
                  _dma(e, wstg[:, (c % 2) * D:(c % 2 + 1) * D],
                       io[wname][c * 128:(c + 1) * 128, :]), LD, 16)
            ld_mark = p.mark(LD)
            p.wait("scalar", LD, ld_mark)
            p.run("scalar", lambda e, i=i, c=c: e.activation(
                out=wsb[i][:, c * D:(c + 1) * D], in_=wstg[:, (c % 2) * D:(c % 2 + 1) * D],
                func=AF.Copy), CV, 1)
            p.wait("sync", CV, p.cnt[CV])
    cv_all = p.mark(CV)
    ldb_all = p.mark(LD)

    mm_marks = []
    for t in range(NTT):
        bi = t % 2
        xn_t = xnt[bi]
        c0 = t * TT
        if t >= 2:
            p.wait("sync", MM, mm_marks[t - 2])
        for fcx in range(FC):
            p.run("sync", lambda e, fcx=fcx, xn_t=xn_t, c0=c0:
                  _dma(e, xn_t[:, fcx * TT:(fcx + 1) * TT],
                       io[src_name][fcx * 128:(fcx + 1) * 128, c0:c0 + TT]), XL, 16)
        xl_mark = p.mark(XL)
        p.wait("tensor", XL, xl_mark)
        if t == 0:
            p.wait("tensor", CV, cv_all)
            p.wait("scalar", LD, ldb_all)
        for i, o in enumerate(outs):
            oname, bA, bB, tmname, donorm = o[0], o[3], o[4], o[5], o[6]
            for jc in range(FC):
                bank = PS[bA] if (jc % 2 == 0) else PS[bB]
                if t > 0 or jc >= 2:
                    p.wait("tensor", EV, max(0, p.cnt[EV] - 1))
                for kc in range(FC):
                    p.run("tensor", lambda e, i=i, jc=jc, kc=kc, bank=bank, xn_t=xn_t: e.matmul(
                        bank[:, :], wsb[i][:, kc * D + jc * 128:kc * D + (jc + 1) * 128],
                        xn_t[:, kc * TT:(kc + 1) * TT],
                        start=(kc == 0), stop=(kc == FC - 1)), MM if kc == FC - 1 else None)
                mmv = p.mark(MM)
                p.wait("scalar", MM, mmv)
                sslot = est[i][:, (jc % 4) * TT:((jc % 4) + 1) * TT]
                if t > 0 or jc >= 4:
                    p.wait("scalar", ST, max(0, p.cnt[ST] - 3 * 16))
                    if tmname is not None:
                        p.wait("scalar", TP, max(0, p.cnt.get(TP, 0) - 3))
                p.run("scalar", lambda e, i=i, jc=jc, bank=bank, sslot=sslot: e.activation(
                    out=sslot, in_=bank[:, :], func=AF.Identity,
                    bias=bcol[i][:, jc:jc + 1], scale=1.0), EV, 1)
                ev_mark = p.mark(EV)
                p.wait("gpsimd", EV, ev_mark)
                p.run("gpsimd", lambda e, i=i, jc=jc, oname=oname, sslot=sslot, c0=c0:
                      _dma(e, io[oname][jc * 128:(jc + 1) * 128, c0:c0 + TT], sslot), ST, 16)
                if tmname is not None:
                    p.wait("tensor", EV, ev_mark)
                    p.wait("tensor", TE, p.cnt.get(TE, 0))
                    for s in range(4):
                        p.run("tensor", lambda e, s=s, sslot=sslot: e.transpose(
                            B4b[0:128, s * 128:(s + 1) * 128],
                            sslot[:, s * 128:(s + 1) * 128], C["idb"][:]),
                            TP if s == 3 else None)
                    tp_mark = p.mark(TP)
                    p.wait("scalar", TP, tp_mark)
                    p.wait("scalar", TS, max(0, p.cnt.get(TS, 0) - 16))
                    p.wait("scalar", NQ, max(0, p.cnt.get(NQ, 0) - 2))
                    tslot = tmstg[:, (jc % 2) * TT:((jc % 2) + 1) * TT]
                    p.run("scalar", lambda e, tslot=tslot: e.activation(
                        out=tslot, in_=B4b[0:128, 0:TT], func=AF.Copy), TE, 1)
                    te_mark = p.mark(TE)
                    p.wait("gpsimd", TE, te_mark)
                    p.run("gpsimd", lambda e, jc=jc, c0=c0, tslot=tslot, tmname=tmname:
                          _dma(e, io[tmname][jc, c0:c0 + TT, :].rearrange("(s q) d -> q s d", q=128),
                               tslot.rearrange("q (s d) -> q s d", s=4)), TS, 16)
                    if donorm:
                        p.wait("vector", TE, te_mark)
                        p.run("vector", lambda e, tslot=tslot: e.tensor_tensor(
                            out=nsq[:], in0=tslot, in1=tslot, op=ALU.mult), NQ, 1)
                        p.run("vector", lambda e, jc=jc, t=t: e.reduce_sum(
                            out=nrm_cols[:, jc * 32 + t * 4:jc * 32 + t * 4 + 4],
                            in_=nsq[:].rearrange("q (s d) -> q s d", s=4), axis=AX.X), NQ, 1)
        mm_marks.append(p.mark(MM))
    p.wait("gpsimd", ST)
    if any_tm:
        p.wait("gpsimd", TS)
    return nrm_cols



NBIS = 48


def emit_sel64(p, io, C, PS, nrm_cols):
    nc = p.nc
    nstg = _sb(nc, "s_nstg", (32, 128), F32)
    nhp = _sb(nc, "s_nhp", (128, TT), F32)
    lo = _sb(nc, "s_lo", (128, 1), F32)
    hi = _sb(nc, "s_hi", (128, 1), F32)
    mid = _sb(nc, "s_mid", (128, 1), F32)
    cnt = _sb(nc, "s_cnt", (128, 1), F32)
    tmpc = _sb(nc, "s_tmpc", (128, 1), F32)
    pred = _sb(nc, "s_pred", (128, 1), F32)
    td = _sb(nc, "s_td", (128, 1), F32)
    cmp_ = _sb(nc, "s_cmp", (128, TT), F32)
    mstg = _sb(nc, "s_mstg", (128, TT), F32)
    mT = _sb(nc, "s_mT", (32, 128), F32)
    cumA = _sb(nc, "s_cumA", (32, 128), F32)
    cumB = _sb(nc, "s_cumB", (32, 128), F32)
    csum = _sb(nc, "s_csum", (32, 1), BF16)
    offs = _sb(nc, "s_offs", (32, 1), F32)
    rankT = _sb(nc, "s_rankT", (32, 128), F32)
    raT = _sb(nc, "s_raT", (32, 128), F32)
    raTb = _sb(nc, "s_raTb", (32, 128), BF16)
    ratm = _sb(nc, "s_ratm", (128, 32), F32)
    beost = _sb(nc, "s_beost", (32, 128), F32)
    ustr = _sb(nc, "s_ustr", (32, 32), BF16)
    B5 = PS[5]

    p.run("sync", lambda e: _dma(e, beost[:], io["beosT"][:, :]), "sL", 16)
    p.run("sync", lambda e: _dma(e, ustr[:], io["ustrict"][:, :]), "sL", 16)

    def vrun(fn):
        p.run("vector", fn, "VSQ", 1)
        p.wait("vector", "VSQ")

    for h in range(LH):
        p.wait("tensor", "sTE", p.cnt.get("sTE", 0))
        p.run("tensor", lambda e, h=h: e.transpose(
            B5[0:32, 0:128], nrm_cols[:, h * 32:(h + 1) * 32], C["idf"][:]), "sNT", 1)
        nt = p.mark("sNT")
        p.wait("scalar", "sNT", nt)
        p.wait("scalar", "sNS", max(0, p.cnt.get("sNS", 0) - 0))
        p.run("scalar", lambda e: e.activation(out=nstg[:], in_=B5[0:32, 0:128], func=AF.Copy),
              "sTE", 1)
        te = p.mark("sTE")
        p.wait("gpsimd", "sTE", te)
        p.run("gpsimd", lambda e, h=h: _dma(
            e, io["nrm"][h:h + 1, :].rearrange("o (tb q) -> (o tb) q", q=128), nstg[:]), "sNS", 16)
    p.wait("sync", "sNS", p.mark("sNS"))
    p.run("sync", lambda e: _dma(e, nhp[:], io["nrm"].rearrange("h (s f) -> (h s) f", f=TT)), "sHP", 16)
    hp_mark = p.mark("sHP")

    p.wait("vector", "sHP", hp_mark)
    vrun(lambda e: e.reduce_max(out=cnt[:], in_=nhp[:], axis=AX.X))
    for d_ in (4, 2, 1):
        mask = [((i & ~7) | ((i + d_) & 7)) for i in range(32)]
        vrun(lambda e, mask=mask: e.stream_shuffle(out=tmpc[:], in_=cnt[:], mask=mask))
        vrun(lambda e: e.tensor_tensor(out=cnt[:], in0=cnt[:], in1=tmpc[:], op=ALU.max))
    vrun(lambda e: e.tensor_scalar(out=hi[:], in0=cnt[:], scalar1=1.0, scalar2=None, op0=ALU.add))
    vrun(lambda e: e.memset(lo[:], 0.0))
    vrun(lambda e: e.tensor_scalar(out=mid[:], in0=hi[:], scalar1=0.5, scalar2=None, op0=ALU.mult))
    for it in range(NBIS):
        vrun(lambda e: e.tensor_scalar(out=cmp_[:], in0=nhp[:], scalar1=mid[:, 0:1],
                                       scalar2=None, op0=ALU.is_gt))
        vrun(lambda e: e.reduce_sum(out=cnt[:], in_=cmp_[:], axis=AX.X))
        for d_ in (4, 2, 1):
            mask = [((i & ~7) | ((i + d_) & 7)) for i in range(32)]
            vrun(lambda e, mask=mask: e.stream_shuffle(out=tmpc[:], in_=cnt[:], mask=mask))
            vrun(lambda e: e.tensor_tensor(out=cnt[:], in0=cnt[:], in1=tmpc[:], op=ALU.add))
        vrun(lambda e: e.tensor_scalar(out=pred[:], in0=cnt[:], scalar1=61.5, scalar2=None,
                                       op0=ALU.is_ge))
        vrun(lambda e: e.tensor_tensor(out=td[:], in0=mid[:], in1=lo[:], op=ALU.subtract))
        vrun(lambda e: e.tensor_tensor(out=td[:], in0=pred[:], in1=td[:], op=ALU.mult))
        vrun(lambda e: e.tensor_tensor(out=lo[:], in0=lo[:], in1=td[:], op=ALU.add))
        vrun(lambda e: e.tensor_tensor(out=td[:], in0=hi[:], in1=mid[:], op=ALU.subtract))
        vrun(lambda e: e.tensor_tensor(out=td[:], in0=pred[:], in1=td[:], op=ALU.mult))
        vrun(lambda e: e.tensor_tensor(out=hi[:], in0=mid[:], in1=td[:], op=ALU.add))
        vrun(lambda e: e.tensor_tensor(out=mid[:], in0=lo[:], in1=hi[:], op=ALU.add))
        vrun(lambda e: e.tensor_scalar(out=mid[:], in0=mid[:], scalar1=0.5, scalar2=None,
                                       op0=ALU.mult))
    vrun(lambda e: e.tensor_scalar(out=mstg[:], in0=nhp[:], scalar1=lo[:, 0:1], scalar2=None,
                                   op0=ALU.is_gt))
    p.wait("sync", "VSQ", p.mark("VSQ"))
    p.run("sync", lambda e: _dma(e, io["msk"].rearrange("h (s f) -> (h s) f", f=TT), mstg[:]),
          "sMS", 16)
    ms_mark = p.mark("sMS")

    p.wait("scalar", "sL")
    for h in range(LH):
        p.wait("sync", "sRA", p.cnt.get("sRA", 0))
        if h == 0:
            p.wait("sync", "sMS", ms_mark)
        p.run("sync", lambda e, h=h: _dma(
            e, mT[:], io["msk"][h:h + 1, :].rearrange("o (tb q) -> (o tb) q", q=128)), "sMT", 16)
        mt_mark = p.mark("sMT")
        p.wait("vector", "sMT", mt_mark)
        vrun(lambda e: e.tensor_tensor(out=mT[:], in0=mT[:], in1=beost[:], op=ALU.max))
        vrun(lambda e: e.tensor_copy(out=cumA[:], in_=mT[:]))
        src_, dst_ = cumA, cumB
        for d_ in (1, 2, 4, 8, 16, 32, 64):
            vrun(lambda e, s=src_, dd=dst_, d_=d_: e.tensor_copy(out=dd[:, 0:d_], in_=s[:, 0:d_]))
            vrun(lambda e, s=src_, dd=dst_, d_=d_: e.tensor_tensor(
                out=dd[:, d_:128], in0=s[:, d_:128], in1=s[:, 0:128 - d_], op=ALU.add))
            src_, dst_ = dst_, src_
        cum = src_
        vrun(lambda e, cum=cum: e.tensor_tensor(out=rankT[:], in0=cum[:], in1=mT[:], op=ALU.subtract))
        vrun(lambda e, cum=cum: e.tensor_copy(out=csum[:], in_=cum[:, 127:128]))
        p.run("vector", lambda e: e.engine_nop(), "sCS", 1)
        cs_mark = p.mark("sCS")
        p.wait("tensor", "sCS", cs_mark)
        p.wait("tensor", "sOF")
        p.run("tensor", lambda e: e.matmul(B5[0:32, 128:129], ustr[:], csum[:],
                                           start=True, stop=True), "sOM", 1)
        om = p.mark("sOM")
        p.wait("vector", "sOM", om)
        vrun(lambda e: e.tensor_copy(out=offs[:], in_=B5[0:32, 128:129]))
        p.run("vector", lambda e: e.engine_nop(), "sOF", 1)
        vrun(lambda e: e.tensor_scalar(out=rankT[:], in0=rankT[:], scalar1=offs[:, 0:1],
                                       scalar2=None, op0=ALU.add))
        vrun(lambda e: e.tensor_tensor(out=raT[:], in0=rankT[:], in1=mT[:], op=ALU.mult))
        vrun(lambda e: e.scalar_tensor_tensor(out=raT[:], in0=raT[:], scalar=-1.0, in1=mT[:],
                                              op0=ALU.add, op1=ALU.add))
        p.run("vector", lambda e: e.engine_nop(), "sRAV", 1)
        rav = p.mark("sRAV")
        p.wait("scalar", "sRAV", rav)
        p.run("scalar", lambda e: e.activation(out=raTb[:], in_=raT[:], func=AF.Copy), "sRB", 1)
        rb = p.mark("sRB")
        p.wait("gpsimd", "sRB", rb)
        p.run("gpsimd", lambda e, h=h: _dma(e, io["rat"][h, :, :], raTb[:]), "sRA", 16)
        p.wait("tensor", "sRAV", rav)
        p.wait("tensor", "sRTE")
        p.run("tensor", lambda e: e.transpose(B5[0:128, 160:192], raT[:], C["idf"][0:32, 0:32]),
              "sRT", 1)
        rt = p.mark("sRT")
        p.wait("scalar", "sRT", rt)
        p.run("scalar", lambda e: e.activation(out=ratm[:], in_=B5[0:128, 160:192], func=AF.Copy),
              "sRTE", 1)
        rte = p.mark("sRTE")
        p.wait("gpsimd", "sRTE", rte)
        p.run("gpsimd", lambda e, h=h: _dma(e, io["ra"][h, :, :], ratm[:]), "sRA", 16)
    p.wait("gpsimd", "sRA")



SQD = float(1.0 / np.sqrt(128.0))
D_HG = 2
D_HH = 8
D_NB = NB
D_STAGE = 9
D_SCATTER = True
D_SERIAL = True
D_BLOCKS = None
D_STORE_EACH = True
D_SCATTER_ZERO = False
D_PT = True
D_LB = None
D_PAR0 = True
D_PCOL0 = False
D_PSLOT0 = False
D_CB3 = False
D_SB0 = False
D_PPT0 = False
DEBUG1D = False


def emit_phase1d(p, io, C, PS, psbf):
    nc = p.nc
    ksb = _sb(nc, "d_ksb", (128, 8 * T), BF16)
    vtsb = _sb(nc, "d_vtsb", (128, 8 * T), BF16)
    qtsb = _sb(nc, "d_qtsb", (128, T), BF16)
    qfb = _sb(nc, "d_qfb", (128, 2 * TT), BF16)
    xhb = _sb(nc, "d_xhb", (128, T), F32)
    gsc = _sb(nc, "d_gsc", (64, T), F32)
    gpb = _sb(nc, "d_gpb", (64, T), BF16)
    gptb = _sb(nc, "d_gptb", (128, 2 * 64), BF16)
    qgb = _sb(nc, "d_qgb", (128, 64), BF16)
    gcb = _sb(nc, "d_gcb", (64, 128), BF16)
    ratrow = _sb(nc, "d_ratrow", (1, T), BF16)
    rasb = _sb(nc, "d_rasb", (128, 8 * 32), F32)
    ptile = _sb(nc, "d_ptile", (128, 2 * 64), BF16)
    ptb = _sb(nc, "d_ptb", (64, 2 * 128), BF16)
    zt64 = _sb(nc, "d_zt64", (64, 128), BF16)
    prtb = [_sb(nc, f"d_prtb{i}", (128, 384), BF16) for i in range(2)]
    prb = _sb(nc, "d_prb", (128, 2 * 256), F32)
    prbb = _sb(nc, "d_prbb", (128, 2 * 256), BF16)
    killc = _sb(nc, "d_killc", (128, 32), F32)
    kc1 = _sb(nc, "d_kc1", (128, 32), F32)
    apst = _sb(nc, "d_apst", (128, 2 * TT), F32)
    dbgt = _sb(nc, "d_dbgt", (128, 256), F32)
    dbgt2 = _sb(nc, "d_dbgt2", (128, 256), F32)
    nmc = _sb(nc, "d_nmc", (128, 2), F32)
    bsc = _sb(nc, "d_bsc", (128, 2), F32)
    rsc = _sb(nc, "d_rsc", (128, 2), F32)
    rrc = _sb(nc, "d_rrc", (128, 2), F32)
    gnm = _sb(nc, "d_gnm", (64, 1), F32)
    gbs = _sb(nc, "d_gbs", (64, 1), F32)
    grs = _sb(nc, "d_grs", (64, 1), F32)
    o64 = _sb(nc, "d_o64", (1, 64), BF16)
    i64c = _sb(nc, "d_i64c", (64, 1), F32)
    iob = _sb(nc, "d_iob", (128, 64), F32)

    B0, B1, B3, B4, B5, B6 = PS[0], PS[1], PS[2], PS[3], PS[4], PS[5]
    B2b = psbf
    B7b = None

    p.run("vector", lambda e: e.memset(zt64[:], 0.0), "dL0", 1)
    p.run("sync", lambda e: _dma(e, o64[:], io["ones64_row"][:, :]), "dL", 16)
    p.run("sync", lambda e: _dma(e, i64c[:], io["iota64_col"][:, :]), "dL", 16)
    p.run("sync", lambda e: _dma(e, iob[:], io["iota64_bc"][:, :]), "dL", 16)
    dl_mark = p.mark("dL")

    def vrun(fn):
        p.run("vector", fn, "VSQ", 1)
        p.wait("vector", "VSQ")

    for hg in range(D_HG):
        p.wait("sync", "dCM", p.cnt.get("dCM", 0))
        p.wait("sync", "dGCi", p.cnt.get("dGCi", 0))
        for hh in range(8):
            h = hg * 8 + hh
            p.run("sync", lambda e, h=h, hh=hh: _dma(
                e, ksb[:, hh * T:(hh + 1) * T], io["k1s"][h * 128:(h + 1) * 128, :]), "dKV", 16)
            p.run("sync", lambda e, h=h, hh=hh: _dma(
                e, vtsb[:, hh * T:(hh + 1) * T].rearrange("q (tb d) -> q tb d", d=128),
                io["v1t"][h, :, :].rearrange("(tb q) d -> q tb d", q=128)), "dKV", 16)
            p.run("sync", lambda e, h=h, hh=hh: _dma(
                e, rasb[:, hh * 32:(hh + 1) * 32], io["ra"][h, :, :]), "dKV", 16)
        kv_mark = p.mark("dKV")

        for hh in range(D_HH):
            h = hg * 8 + hh
            kh = ksb[:, hh * T:(hh + 1) * T]
            vth = vtsb[:, hh * T:(hh + 1) * T]
            rah = rasb[:, hh * 32:(hh + 1) * 32]
            p.wait("sync", "dGQ", p.cnt.get("dGQ", 0))
            p.wait("sync", "dPM", p.cnt.get("dPM", 0))
            p.wait("sync", "dCX", p.cnt.get("dCX", 0))
            p.run("sync", lambda e, h=h: _dma(
                e, qtsb[:].rearrange("q (tb d) -> q tb d", d=128),
                io["q1t"][h, :, :].rearrange("(tb q) d -> q tb d", q=128)), "dQTL", 16)
            p.run("sync", lambda e, h=h: _dma(
                e, ratrow[:], io["rat"][h, :, :].rearrange("a b -> (a b)")[None, :]
                if False else io["rat"].rearrange("h a b -> h (a b)")[h:h + 1, :]), "dQTL", 16)
            p.run("sync", lambda e, h=h: _dma(
                e, xhb[:], io["x1"][h * 128:(h + 1) * 128, :]), "dQTL", 16)
            qtl_mark = p.mark("dQTL")

            if D_STAGE < 1:
                continue
            p.wait("vector", "dKV", kv_mark)
            vrun(lambda e, rah=rah: e.tensor_scalar(out=killc[:], in0=rah, scalar1=-0.5,
                                                    scalar2=None, op0=ALU.is_gt))
            vrun(lambda e, rah=rah: e.tensor_scalar(out=kc1[:], in0=rah, scalar1=63.5,
                                                    scalar2=None, op0=ALU.is_lt))
            vrun(lambda e: e.tensor_tensor(out=killc[:], in0=killc[:], in1=kc1[:], op=ALU.mult))
            vrun(lambda e: e.tensor_scalar(out=killc[:], in0=killc[:], scalar1=-1.0, scalar2=1.0,
                                           op0=ALU.mult, op1=ALU.add))

            if D_STAGE < 2:
                continue
            p.wait("vector", "dQTL", qtl_mark)
            p.wait("vector", "dGQ", p.cnt.get("dGQ", 0))
            for tb in range(32):
                r = tb % 2
                if tb >= 2:
                    p.wait("vector", "dGQi", p.cnt.get("dGQi", 0) - 1)
                p.run("vector", lambda e, tb=tb, r=r, rah=rah: e.tensor_scalar(
                    out=ptile[:, r * 64:(r + 1) * 64], in0=iob[:], scalar1=rah[:, tb:tb + 1],
                    scalar2=None, op0=ALU.is_equal), "dPT", 1)
                p.wait("tensor", "dPT")
                if tb == 0:
                    p.wait("tensor", "dL", dl_mark)
                    p.wait("tensor", "dGE", p.cnt.get("dGE", 0))
                    p.wait("tensor", "dGCB", p.cnt.get("dGCB", 0))
                p.run("tensor", lambda e, tb=tb, r=r: e.matmul(
                    B6[0:128, 0:64], qtsb[:, tb * 128:(tb + 1) * 128],
                    ptile[:, r * 64:(r + 1) * 64],
                    start=(tb == 0), stop=(tb == 31)), "dGQi", 1)
                if tb == 31:
                    p.run("tensor", lambda e: e.nop(), "dGQ", 1)
            gq_mark = p.mark("dGQ")
            p.wait("scalar", "dGQ", gq_mark)
            p.run("scalar", lambda e: e.activation(out=qgb[:, 0:64], in_=B6[0:128, 0:64],
                                                   func=AF.Copy), "dQG", 1)
            qg_mark = p.mark("dQG")
            if D_STAGE < 2.5:
                continue
            p.wait("tensor", "dQG", qg_mark)
            for kc8 in range(8):
                gbank = B1 if kc8 % 2 == 0 else B6
                p.wait("tensor", "dGE", max(0, p.cnt.get("dGE", 0) - 1))
                p.run("tensor", lambda e, kc8=kc8, kh=kh, gbank=gbank: e.matmul(
                    gbank[0:64, 0:TT], qgb[:, 0:64], kh[:, kc8 * TT:(kc8 + 1) * TT],
                    start=True, stop=True), "dGS", 1)
                gs_mark = p.mark("dGS")
                p.wait("scalar", "dGS", gs_mark)
                p.run("scalar", lambda e, kc8=kc8, gbank=gbank: e.activation(
                    out=gsc[:, kc8 * TT:(kc8 + 1) * TT], in_=gbank[0:64, 0:TT], func=AF.Copy),
                    "dGE", 1)
            ge_mark = p.mark("dGE")
            if D_STAGE < 3:
                continue
            p.wait("vector", "dGE", ge_mark)
            vrun(lambda e: e.reduce_max(out=gnm[:], in_=gsc[:], axis=AX.X, negate=True))
            vrun(lambda e: e.tensor_scalar(out=gbs[:], in0=gnm[:], scalar1=SQD, scalar2=None,
                                           op0=ALU.mult))
            p.run("vector", lambda e: e.engine_nop(), "dGB", 1)
            gb_mark = p.mark("dGB")
            if D_STAGE < 3.5:
                continue
            p.wait("scalar", "dGB", gb_mark)
            p.run("scalar", lambda e: e.activation(out=gpb[:], in_=gsc[:], func=AF.Exp,
                                                   bias=gbs[:, 0:1], scale=SQD,
                                                   accum_out=grs[:, 0:1]), "dGX", 1)
            gx_mark = p.mark("dGX")
            if D_STAGE < 3.7:
                continue
            p.wait("vector", "dGX", gx_mark)
            vrun(lambda e: e.reciprocal(out=grs[:], in_=grs[:]))
            vrun(lambda e: e.tensor_scalar(out=gpb[:], in0=gpb[:], scalar1=grs[:, 0:1],
                                           scalar2=None, op0=ALU.mult))
            p.run("vector", lambda e: e.engine_nop(), "dGP", 1)
            gp_mark = p.mark("dGP")
            if D_STAGE < 4:
                continue
            p.wait("tensor", "dGP", gp_mark)
            for tb in range(32):
                r = tb % 2
                if tb >= 2:
                    p.wait("tensor", "dGCi", p.cnt.get("dGCi", 0) - 1)
                p.run("tensor", lambda e, tb=tb, r=r: e.transpose(
                    psbf[0:128, 768 + r * 64:768 + (r + 1) * 64], gpb[:, tb * 128:(tb + 1) * 128],
                    C["idb"][0:64, 0:64]), "dGT", 1)
                gt_mark = p.mark("dGT")
                p.wait("scalar", "dGT", gt_mark)
                p.run("scalar", lambda e, r=r: e.activation(
                    out=gptb[:, r * 64:(r + 1) * 64], in_=psbf[0:128, 768 + r * 64:768 + (r + 1) * 64],
                    func=AF.Copy), "dGTE", 1)
                gte_mark = p.mark("dGTE")
                p.wait("tensor", "dGTE", gte_mark)
                p.run("tensor", lambda e, tb=tb, r=r, vth=vth: e.matmul(
                    B6[0:64, 0:128], gptb[:, r * 64:(r + 1) * 64],
                    vth[:, tb * 128:(tb + 1) * 128],
                    start=(tb == 0), stop=(tb == 31)), "dGCi", 1)
            gc_mark = p.mark("dGCi")
            p.wait("scalar", "dGCi", gc_mark)
            p.run("scalar", lambda e: e.activation(out=gcb[:], in_=B6[0:64, 0:128],
                                                   func=AF.Copy), "dGCB", 1)
            gcb_mark = p.mark("dGCB")

            if D_STAGE < 5:
                continue
            blocks = D_BLOCKS if D_BLOCKS is not None else range(D_NB)
            for b in blocks:
                par = (b % 2) if not D_PAR0 else 0
                pcol = par if not D_PCOL0 else 0
                pslot = par if not D_PSLOT0 else 0
                ppt = par if not D_PPT0 else 0
                if D_SERIAL:
                    for s_ in ("dCX", "dEX", "dPE", "dCM", "dPS", "dSM", "dTR", "dPM", "dPB"):
                        for e_ in ("tensor", "vector", "scalar"):
                            p.wait(e_, s_, p.cnt.get(s_, 0))
                sbank = (B0 if par == 0 else B1) if not D_SB0 else B0
                cbank = (B4 if par == 0 else B3) if not D_CB3 else B3
                prt = prtb[pslot]
                if b % 4 == 0 or b == list(blocks)[0]:
                    if b >= 8:
                        p.wait("sync", "dSM", max(0, p.cnt.get("dSM", 0) - 4))
                    g4 = (b // 4) * 4
                    p.run("sync", lambda e, h=h, g4=g4: _dma(
                        e, qfb[:, (g4 // 4 % 2) * TT:((g4 // 4 % 2) + 1) * TT],
                        io["q1s"][h * 128:(h + 1) * 128, g4 * 128:(g4 + 4) * 128]), "dQF", 16)
                qf_mark = p.mark("dQF")
                w0 = max(0, b * 128 - 64)
                w1 = min(T, b * 128 + 192)
                wid = w1 - w0
                segs = []
                col = 0
                if b > 0:
                    segs.append((b - 1, 64, 64, 0))
                    col = 64
                segs.append((b, 0, 128, col))
                if b < NB - 1:
                    segs.append((b + 1, 0, 64, col + 128))
                p.wait("tensor", "dQF", qf_mark)
                p.wait("tensor", "dEX", max(0, p.cnt.get("dEX", 0) - 1))
                qsl = qfb[:, (b // 4 % 2) * TT + (b % 4) * 128:(b // 4 % 2) * TT + (b % 4 + 1) * 128]
                p.run("tensor", lambda e, qsl=qsl, kh=kh, w0=w0, wid=wid, sbank=sbank: e.matmul(
                    sbank[0:128, 0:wid], qsl, kh[:, w0:w0 + wid], start=True, stop=True), "dSM", 1)
                sm_mark = p.mark("dSM")
                if DEBUG1D and h == 0 and b == 2:
                    p.wait("scalar", "dSM", sm_mark)
                    p.run("scalar", lambda e, sbank=sbank: e.activation(
                        out=dbgt2[:], in_=sbank[0:128, 0:256], func=AF.Copy), "dDB3", 1)
                    p.wait("gpsimd", "dDB3", p.mark("dDB3"))
                    p.run("gpsimd", lambda e: _dma(e, io["dbg5"][0:128, :], dbgt2[:]), "dDBG", 16)
                    p.wait("vector", "dSM", sm_mark)
                    p.run("vector", lambda e, sbank=sbank: e.tensor_copy(
                        out=dbgt[:], in_=sbank[0:128, 0:256]), "dDBE", 1)
                    p.wait("gpsimd", "dDBE", p.mark("dDBE"))
                    p.run("gpsimd", lambda e: _dma(e, io["dbg3"][0:128, 0:256], dbgt[:]), "dDBG", 16)
                    p.run("gpsimd", lambda e, qsl=qsl: _dma(e, io["dbg4"][0:128, 0:128], qsl), "dDBG", 16)
                    p.run("gpsimd", lambda e, kh=kh, w0=w0: _dma(
                        e, io["dbg4"][0:128, 128:384], kh[:, w0:w0 + 256]), "dDBG", 16)
                p.wait("vector", "dSM", sm_mark)
                vrun(lambda e, sbank=sbank, wid=wid, par=par: e.reduce_max(
                    out=nmc[:, pcol:pcol + 1], in_=sbank[0:128, 0:wid], axis=AX.X, negate=True))
                vrun(lambda e, par=par: e.tensor_scalar(
                    out=bsc[:, pcol:pcol + 1], in0=nmc[:, pcol:pcol + 1], scalar1=SQD,
                    scalar2=None, op0=ALU.mult))
                p.run("vector", lambda e: e.engine_nop(), "dBS", 1)
                bs_mark = p.mark("dBS")
                p.wait("scalar", "dBS", bs_mark)
                p.run("scalar", lambda e, sbank=sbank, wid=wid, par=par: e.activation(
                    out=prb[:, pslot * 256:pslot * 256 + wid], in_=sbank[0:128, 0:wid], func=AF.Exp,
                    bias=bsc[:, pcol:pcol + 1], scale=SQD, accum_out=rsc[:, pcol:pcol + 1]), "dEX", 1)
                ex_mark = p.mark("dEX")
                p.wait("vector", "dEX", ex_mark)
                vrun(lambda e, pcol=pcol: e.reciprocal(out=rrc[:, pcol:pcol + 1],
                                                     in_=rsc[:, pcol:pcol + 1]))
                vrun(lambda e, pcol=pcol, b=b: e.tensor_tensor(
                    out=rrc[:, pcol:pcol + 1], in0=rrc[:, pcol:pcol + 1],
                    in1=killc[:, b:b + 1], op=ALU.mult))
                vrun(lambda e, pslot=pslot, pcol=pcol, wid=wid: e.tensor_scalar(
                    out=prbb[:, pslot * 256:pslot * 256 + wid], in0=prb[:, pslot * 256:pslot * 256 + wid],
                    scalar1=rrc[:, pcol:pcol + 1], scalar2=None, op0=ALU.mult))
                p.run("vector", lambda e: e.engine_nop(), "dPS", 1)
                ps_mark = p.mark("dPS")
                if DEBUG1D and h == 0 and b < 2:
                    p.wait("gpsimd", "dPS", ps_mark)
                    p.run("gpsimd", lambda e, b=b, par=par: _dma(
                        e, io["dbg1"][b * 128:(b + 1) * 128, 0:256],
                        prb[:, par * 256:(par + 1) * 256]), "dDBG", 16)
                    p.run("gpsimd", lambda e, b=b, par=par: e.dma_start(
                        out=io["dbg1"][b * 128:(b + 1) * 128, 256:257], in_=rrc[:, par:par + 1],
                        allow_slow_non_contiguous=True), "dDBG", 16)
                    p.run("gpsimd", lambda e, b=b, par=par: e.dma_start(
                        out=io["dbg1"][b * 128:(b + 1) * 128, 258:259], in_=nmc[:, par:par + 1],
                        allow_slow_non_contiguous=True), "dDBG", 16)
                p.wait("tensor", "dPS", ps_mark)
                p.wait("tensor", "dPE", max(0, p.cnt.get("dPE", 0) - 1))
                part = par
                order = sorted(range(len(segs)), key=lambda i: segs[i][2])
                for oi, si in enumerate(order):
                    kt, r0, w, pc0 = segs[si]
                    p.run("tensor", lambda e, si=si, w=w, pc0=pc0, pslot=pslot: e.transpose(
                        B2b[0:w, pslot * 384 + si * 128:pslot * 384 + si * 128 + 128],
                        prbb[:, pslot * 256 + pc0:pslot * 256 + pc0 + w],
                        C["idb"][:]), "dTR" if oi == len(segs) - 1 else None)
                tr_mark = p.mark("dTR")
                p.wait("scalar", "dTR", tr_mark)
                p.wait("scalar", "dCM", max(0, p.cnt.get("dCM", 0) - 1))
                for si, (kt, r0, w, pc0) in enumerate(segs):
                    p.run("scalar", lambda e, si=si, w=w, r0=r0, pslot=pslot, prt=prt: e.activation(
                        out=prt[r0:r0 + w, si * 128:(si + 1) * 128],
                        in_=B2b[0:w, pslot * 384 + si * 128:pslot * 384 + si * 128 + 128],
                        func=AF.Copy), "dPE" if si == len(segs) - 1 else None, 1)
                pe_mark = p.mark("dPE")
                if D_LB is not None and b == list(blocks)[-1] and D_LB < 3:
                    continue
                if not D_PT:
                    p.run("vector", lambda e: e.engine_nop(), "dPB", 1)
                p.wait("tensor", "dPB", max(0, p.cnt.get("dPB", 0) - 1))
                if not D_PT:
                    pass
                elif True:
                    pass
                if D_PT:
                    p.run("tensor", lambda e, b=b, ppt=ppt: e.matmul(
                        B5[0:64, ppt * 128:(ppt + 1) * 128], o64[:],
                        ratrow[0:1, b * 128:(b + 1) * 128], start=True, stop=True), "dPM", 1)
                    pm_mark = p.mark("dPM")
                    p.wait("vector", "dPM", pm_mark)
                    p.wait("vector", "dCM", max(0, p.cnt.get("dCM", 0) - 1))
                    vrun(lambda e, ppt=ppt: e.tensor_scalar(
                        out=ptb[:, ppt * 128:(ppt + 1) * 128],
                        in0=B5[0:64, ppt * 128:(ppt + 1) * 128],
                        scalar1=i64c[:, 0:1], scalar2=None, op0=ALU.is_equal))
                    p.run("vector", lambda e: e.engine_nop(), "dPB", 1)
                pb_mark = p.mark("dPB")
                if D_LB is not None and b == list(blocks)[-1] and D_LB < 4:
                    continue
                p.wait("tensor", "dPE", pe_mark)
                p.wait("tensor", "dPB", pb_mark)
                if b < 2:
                    p.wait("tensor", "dGCB", gcb_mark)
                    p.wait("tensor", "dL0", p.cnt.get("dL0", 0))
                p.wait("tensor", "dCX", max(0, p.cnt.get("dCX", 0) - 1))
                nseg = len(segs)
                nsegs = len(segs)
                for si, (kt, r0, w, pc0) in enumerate(segs):
                    p.run("tensor", lambda e, si=si, kt=kt, r0=r0, w=w, vth=vth, cbank=cbank, prt=prt,
                          last=(si == nsegs - 1 and not D_SCATTER): e.matmul(
                        cbank[0:128, 0:128], vth[r0:r0 + w, kt * 128:(kt + 1) * 128],
                        prt[r0:r0 + w, si * 128:(si + 1) * 128],
                        start=(si == 0), stop=last),
                        "dCM" if (si == nsegs - 1 and not D_SCATTER) else None)
                if D_SCATTER:
                    rhs_sel = (lambda ppt=ppt: ptb[:, ppt * 128:(ppt + 1) * 128])() \
                        if not D_SCATTER_ZERO else zt64[:]
                    p.run("tensor", lambda e, cbank=cbank, rhs_sel=rhs_sel: e.matmul(
                        cbank[0:128, 0:128], gcb[:], rhs_sel,
                        start=False, stop=True), "dCM", 1)
                cm_mark = p.mark("dCM")
                if DEBUG1D and h == 0 and b == 2:
                    p.wait("gpsimd", "dPE", pe_mark)
                    p.run("gpsimd", lambda e, prt=prt: _dma(e, io["dbg2"][0:128, 0:384], prt[:]), "dDBG", 16)
                    p.run("gpsimd", lambda e, par=par: _dma(e, io["dbg3"][0:128, 0:256],
                                                            prbb[:, par * 256:(par + 1) * 256]), "dDBG", 16)
                    p.run("gpsimd", lambda e, par=par: _dma(e, io["dbg2"][0:64, 384:512],
                                                            ptb[0:64, par * 128:(par + 1) * 128]), "dDBG", 16)

                p.wait("vector", "dCM", cm_mark)
                p.wait("vector", "dQTL", qtl_mark)
                if b >= 8:
                    p.wait("vector", "dAS", p.cnt.get("dAS", 0) - 16)
                p.run("vector", lambda e, b=b, cbank=cbank: e.tensor_tensor(
                    out=apst[:, (b % 8) * 128:(b % 8 + 1) * 128], in0=cbank[0:128, 0:128],
                    in1=xhb[:, b * 128:(b + 1) * 128], op=ALU.add), "dCX", 1)
                cx_mark = p.mark("dCX")
                if D_STORE_EACH:
                    p.wait("sync", "dCX", cx_mark)
                    p.run("sync", lambda e, h=h, b=b: _dma(
                        e, io["ap1"][h * 128:(h + 1) * 128, b * 128:(b + 1) * 128],
                        apst[:, (b % 8) * 128:(b % 8 + 1) * 128]), "dAS", 16)
                elif b % 4 == 3:
                    p.wait("sync", "dCX", cx_mark)
                    p.run("sync", lambda e, h=h, b=b: _dma(
                        e, io["ap1"][h * 128:(h + 1) * 128, (b - 3) * 128:(b + 1) * 128],
                        apst[:, (b // 4 % 2) * TT:(b // 4 % 2 + 1) * TT]), "dAS", 16)
    p.wait("gpsimd", "dAS")


def emit_phase0c(p, io, C, PS, psbf):
    nc = p.nc
    g = _gammas()
    wv = _sb(nc, "c_wv", (128, FC * D), BF16)
    wstg = _sb(nc, "c_wstg", (128, D // 4), F32)
    bvcol = _sb(nc, "c_bv", (128, FC), F32)
    qbuf = _sb(nc, "c_qbuf", (128, FC * 2 * CH), BF16)
    kbuf = _sb(nc, "c_kbuf", (128, FC * 2 * CH), BF16)
    xnc = kbuf
    pbuf = [_sb(nc, "c_pbuf", (128, FC * CH), F32)] * 2
    vbf = [_sb(nc, f"c_vbf{i}", (128, FC * 2 * CH), BF16) for i in range(2)]
    S32 = _sb(nc, "c_S32", (128, RH * 2 * CH), F32)
    Sbf = _sb(nc, "c_Sbf", (128, RH * 2 * CH), BF16)
    dtsb = _sb(nc, "c_dt", (128, RH * 2 * CH), BF16)
    csb = _sb(nc, "c_cs", (128, RH * CH), BF16)
    kvb = _sb(nc, "c_kv", (128, RH * CH), BF16)
    qhat = _sb(nc, "c_qhat", (128, 2 * CH), BF16)
    khat = _sb(nc, "c_khat", (128, 2 * CH), BF16)
    amb = _sb(nc, "c_amb", (128, 2 * CH), BF16)
    vtm = _sb(nc, "c_vtm", (128, 2 * CH), BF16)
    khtm = _sb(nc, "c_khtm", (128, 2 * CH), BF16)
    x1st = _sb(nc, "c_x1st", (128, 2 * CH), F32)

    B0, B1, B3, B4, B5 = PS[0], PS[1], PS[2], PS[3], PS[4]
    B2b = psbf

    for c in range(FC):
        p.run("sync", lambda e, c=c: _dma(e, bvcol[:, c:c + 1],
                                          io["l0bv"][c * 128:(c + 1) * 128].rearrange("(p o) -> p o", o=1)),
              "cL", 16)
    for c in range(FC):
        for hlf in range(4):
            p.run("sync", lambda e, c=c, hlf=hlf: _dma(
                e, wstg[:], io["l0wv"][c * 128:(c + 1) * 128, hlf * (D // 4):(hlf + 1) * (D // 4)]),
                "cL", 16)
            lm = p.mark("cL")
            p.wait("scalar", "cL", lm)
            p.run("scalar", lambda e, c=c, hlf=hlf: e.activation(
                out=wv[:, c * D + hlf * (D // 4):c * D + (hlf + 1) * (D // 4)], in_=wstg[:],
                func=AF.Copy), "cC", 1)
            if not (c == FC - 1 and hlf == 3):
                p.wait("sync", "cC", p.cnt["cC"])
    for h in range(RH):
        for js in range(2):
            p.run("scalar", lambda e, h=h, js=js: _dma(
                e, dtsb[:, (h * 2 + js) * CH:(h * 2 + js + 1) * CH],
                io["dt_bf"][h, js * 128:(js + 1) * 128, :]), "cL2", 16)
        p.run("scalar", lambda e, h=h: _dma(
            e, csb[:, h * CH:(h + 1) * CH],
            io["cs_f32"][h:h + 1, :].broadcast_to((128, CH))), "cL2", 16)
        p.run("scalar", lambda e, h=h: _dma(
            e, kvb[:, h * CH:(h + 1) * CH],
            io["kv_f32"][h:h + 1, :].broadcast_to((128, CH))), "cL2", 16)
    p.run("vector", lambda e: e.memset(S32[:], 0.0), "cSN", 1)
    p.run("vector", lambda e: e.memset(Sbf[:], 0.0), "cSB0", 1)
    sb0 = p.mark("cSB0")
    cv_all = p.mark("cC")
    ld_all = p.mark("cL")
    ld2_all = p.mark("cL2")

    cd = [float(g[h] ** CH) for h in range(RH)]
    vb_ct_marks = {}

    for pair in range(NCH // 2):
        t0 = pair * 2 * CH
        vb = vbf[pair % 2]
        if pair > 0:
            p.wait("sync", "cA", p.cnt["cA"])
        for fcx in range(FC):
            p.run("sync", lambda e, fcx=fcx, t0=t0: _dma(
                e, qbuf[:, fcx * 512:(fcx + 1) * 512],
                io["q0s"][fcx * 128:(fcx + 1) * 128, t0:t0 + 512]), "cQK", 16)
            p.run("sync", lambda e, fcx=fcx, t0=t0: _dma(
                e, xnc[:, fcx * 512:(fcx + 1) * 512],
                io["xn0"][fcx * 128:(fcx + 1) * 128, t0:t0 + 512]), "cXN", 16)
        xn_mark = p.mark("cXN")

        if pair == 0:
            p.wait("tensor", "cC", cv_all)
            p.wait("scalar", "cL", ld_all)
            p.wait("vector", "cL2", ld2_all)
        p.wait("tensor", "cXN", xn_mark)
        if pair >= 2:
            p.wait("scalar", "cT", vb_ct_marks[pair - 2])
        vbank = [B0, B1]
        for jc in range(FC):
            bank = vbank[jc % 2]
            if pair > 0 or jc >= 2:
                p.wait("tensor", "cE", max(0, p.cnt["cE"] - 1))
            for kc in range(FC):
                p.run("tensor", lambda e, jc=jc, kc=kc, bank=bank: e.matmul(
                    bank[:, :], wv[:, kc * D + jc * 128:kc * D + (jc + 1) * 128],
                    xnc[:, kc * 512:(kc + 1) * 512],
                    start=(kc == 0), stop=(kc == FC - 1)), "cVM" if kc == FC - 1 else None)
            vm = p.mark("cVM")
            p.wait("scalar", "cVM", vm)
            p.run("scalar", lambda e, jc=jc, bank=bank, vb=vb: e.activation(
                out=vb[:, jc * 512:(jc + 1) * 512], in_=bank[:, :], func=AF.Identity,
                bias=bvcol[:, jc:jc + 1], scale=1.0), "cE", 1)
        ve_mark = p.mark("cE")
        p.wait("sync", "cVM", p.mark("cVM"))
        for fcx in range(FC):
            p.run("sync", lambda e, fcx=fcx, t0=t0: _dma(
                e, kbuf[:, fcx * 512:(fcx + 1) * 512],
                io["k0s"][fcx * 128:(fcx + 1) * 128, t0:t0 + 512]), "cQK", 16)
        qk_mark = p.mark("cQK")

        for ci in range(2):
            c = pair * 2 + ci
            cc0 = ci * CH
            pb = pbuf[c % 2]
            if c >= 1:
                p.wait("sync", "cXV", 16 * c)
            for fcx in range(FC):
                p.run("sync", lambda e, fcx=fcx, c=c, pb=pb: _dma(
                    e, pb[:, fcx * CH:(fcx + 1) * CH],
                    io["p0"][fcx * 128:(fcx + 1) * 128, c * CH:(c + 1) * CH]), "cP", 16)
            p_mark = p.mark("cP")

            for h in range(RH):
                q0sl = [qbuf[:, (2 * h + l) * 512 + cc0:(2 * h + l) * 512 + cc0 + CH] for l in range(2)]
                k0sl = [kbuf[:, (2 * h + l) * 512 + cc0:(2 * h + l) * 512 + cc0 + CH] for l in range(2)]
                vsl = [vb[:, (2 * h + l) * 512 + cc0:(2 * h + l) * 512 + cc0 + CH] for l in range(2)]
                p.wait("vector", "cQK", qk_mark)
                p.wait("vector", "cO")
                p.wait("vector", "cT")
                for l in range(2):
                    p.run("vector", lambda e, l=l, h=h, q0=q0sl[l]: e.tensor_tensor(
                        out=qhat[:, l * CH:(l + 1) * CH], in0=q0,
                        in1=csb[:, h * CH:(h + 1) * CH], op=ALU.mult))
                    p.run("vector", lambda e, l=l, h=h, k0=k0sl[l]: e.tensor_tensor(
                        out=khat[:, l * CH:(l + 1) * CH], in0=k0,
                        in1=kvb[:, h * CH:(h + 1) * CH], op=ALU.mult),
                        "cQH" if l == 1 else None, 1)
                qh_mark = p.mark("cQH")
                p.wait("tensor", "cE", ve_mark)
                p.wait("tensor", "cQH", qh_mark)
                p.wait("tensor", "cTE")
                for l in range(2):
                    for s in range(2):
                        p.run("tensor", lambda e, l=l, s=s, vv=vsl[l]: e.transpose(
                            B2b[0:128, s * 256 + l * 128:s * 256 + (l + 1) * 128],
                            vv[:, s * 128:(s + 1) * 128], C["idb"][:]))
                for l in range(2):
                    for s in range(2):
                        p.run("tensor", lambda e, l=l, s=s: e.transpose(
                            B2b[0:128, 512 + s * 256 + l * 128:512 + s * 256 + (l + 1) * 128],
                            khat[:, l * CH + s * 128:l * CH + (s + 1) * 128],
                            C["idb"][:]), "cT" if (l, s) == (1, 1) else None)
                tp_mark = p.mark("cT")
                p.wait("scalar", "cT", tp_mark)
                p.wait("scalar", "cO")
                p.wait("scalar", "cS")
                p.run("scalar", lambda e: e.activation(out=vtm[:], in_=B2b[0:128, 0:512],
                                                       func=AF.Copy))
                p.run("scalar", lambda e: e.activation(out=khtm[:], in_=B2b[0:128, 512:1024],
                                                       func=AF.Copy), "cTE", 1)
                te_mark = p.mark("cTE")
                p.wait("tensor", "cAM")
                for s in range(2):
                    for l in range(2):
                        p.run("tensor", lambda e, s=s, l=l, kk=k0sl[l], qq=q0sl[l]: e.matmul(
                            B3[:, s * CH:(s + 1) * CH], kk[:, s * 128:(s + 1) * 128], qq,
                            start=(l == 0), stop=(l == 1)), "cA" if (s, l) == (1, 1) else None)
                a_mark = p.mark("cA")
                p.wait("vector", "cA", a_mark)
                p.wait("vector", "cO")
                for s in range(2):
                    p.run("vector", lambda e, s=s, h=h: e.tensor_tensor(
                        out=amb[:, s * CH:(s + 1) * CH], in0=B3[:, s * CH:(s + 1) * CH],
                        in1=dtsb[:, (h * 2 + s) * CH:(h * 2 + s + 1) * CH], op=ALU.mult),
                        "cAM" if s == 1 else None, 1)
                am_mark = p.mark("cAM")
                p.wait("tensor", "cAM", am_mark)
                p.wait("tensor", "cTE", te_mark)
                p.wait("tensor", "cXV")
                if c > 0:
                    p.wait("tensor", "cSB", 16 * (c - 1) + 2 * (h + 1))
                else:
                    p.wait("tensor", "cSB0", sb0)
                for ec in range(2):
                    for s in range(2):
                        p.run("tensor", lambda e, ec=ec, s=s: e.matmul(
                            B4[:, ec * CH:(ec + 1) * CH],
                            vtm[:, s * 256 + ec * 128:s * 256 + (ec + 1) * 128],
                            amb[:, s * CH:(s + 1) * CH],
                            start=(s == 0), stop=False))
                    for l in range(2):
                        p.run("tensor", lambda e, ec=ec, l=l, h=h: e.matmul(
                            B4[:, ec * CH:(ec + 1) * CH],
                            Sbf[:, (h * 2 + l) * CH + ec * 128:(h * 2 + l) * CH + (ec + 1) * 128],
                            qhat[:, l * CH:(l + 1) * CH],
                            start=False, stop=(l == 1)), "cO" if (ec, l) == (1, 1) else None)
                o_mark = p.mark("cO")
                p.wait("tensor", "cSN")
                for dc in range(2):
                    for s in range(2):
                        p.run("tensor", lambda e, dc=dc, s=s: e.matmul(
                            B5[:, dc * CH:(dc + 1) * CH],
                            khtm[:, s * 256 + dc * 128:s * 256 + (dc + 1) * 128],
                            vtm[:, s * 256:(s + 1) * 256],
                            start=(s == 0), stop=(s == 1)), "cS" if (dc, s) == (1, 1) else None)
                s_mark = p.mark("cS")
                p.wait("vector", "cO", o_mark)
                p.wait("vector", "cP", p_mark)
                p.wait("vector", "cST")
                for ec in range(2):
                    p.run("vector", lambda e, ec=ec, h=h, pb=pb: e.tensor_tensor(
                        out=x1st[:, ec * CH:(ec + 1) * CH], in0=B4[:, ec * CH:(ec + 1) * CH],
                        in1=pb[:, (2 * h + ec) * CH:(2 * h + ec + 1) * CH], op=ALU.add),
                        "cXV", 1)
                xv_mark = p.mark("cXV")
                p.wait("sync", "cXV", xv_mark)
                for ec in range(2):
                    p.run("sync", lambda e, ec=ec, h=h, c=c: _dma(
                        e, io["x1"][(2 * h + ec) * 128:(2 * h + ec + 1) * 128, c * CH:(c + 1) * CH],
                        x1st[:, ec * CH:(ec + 1) * CH]), "cST", 16)
                p.wait("vector", "cS", s_mark)
                for dc in range(2):
                    p.run("vector", lambda e, dc=dc, h=h: e.scalar_tensor_tensor(
                        out=S32[:, (h * 2 + dc) * CH:(h * 2 + dc + 1) * CH],
                        in0=S32[:, (h * 2 + dc) * CH:(h * 2 + dc + 1) * CH],
                        scalar=cd[h], in1=B5[:, dc * CH:(dc + 1) * CH],
                        op0=ALU.mult, op1=ALU.add), "cSN", 1)
                sn_mark = p.mark("cSN")
                p.wait("scalar", "cO", o_mark)
                p.wait("scalar", "cSN", sn_mark)
                for dc in range(2):
                    p.run("scalar", lambda e, dc=dc, h=h: e.activation(
                        out=Sbf[:, (h * 2 + dc) * CH:(h * 2 + dc + 1) * CH],
                        in_=S32[:, (h * 2 + dc) * CH:(h * 2 + dc + 1) * CH],
                        func=AF.Copy), "cSB", 1)
        vb_ct_marks[pair] = p.mark("cT")
    p.wait("gpsimd", "cST")
    p.wait("gpsimd", "cSB")



def _sb_at(nc, name, shape, dtype, off):
    return nc.alloc_sbuf_tensor_at(name, list(shape), dtype, offset=off).ap()


ARENA = 32704
DEBUG1E = False
E_SERIAL = False
E_HG = 2
E_NT = NTT


def emit_phase1e(p, io, C, PS):
    nc = p.nc
    wgtb = _sb_at(nc, "e_wgtb", (128, 16 * 1024), BF16, ARENA)
    wutb = _sb_at(nc, "e_wutb", (128, 16 * 1024), BF16, ARENA + 32768)
    xnr = _sb_at(nc, "e_xnr", (128, 2 * FC * TT), BF16, ARENA + 65536)
    hstg = _sb_at(nc, "e_hstg", (128, 4 * TT), BF16, ARENA + 98304)
    rawr = _sb_at(nc, "e_rawr", (128, 2 * 4096), I32, ARENA + 102400)
    deqr = _sb_at(nc, "e_deqr", (128, 2 * 4096), BF16, ARENA + 135168)
    wdtb = _sb_at(nc, "e_wdtb", (128, 64 * 1024), BF16, ARENA)
    hab = _sb_at(nc, "e_hab", (128, 2 * 64 * 256), BF16, ARENA + 131072)
    gcol = _sb(nc, "e_gcol", (128, 64), F32)
    ucol = _sb(nc, "e_ucol", (128, 64), F32)
    dcol = _sb(nc, "e_dcol", (128, 16), F32)
    gofs = _sb(nc, "e_gofs", (128, 64), F32)
    uofs = _sb(nc, "e_uofs", (128, 64), F32)
    dofs = _sb(nc, "e_dofs", (128, 16), F32)
    apt = _sb(nc, "e_apt", (128, 2 * TT), F32)
    ost = _sb(nc, "e_ost", (128, 2 * TT), F32)

    for c in range(64):
        p.run("sync", lambda e, c=c: _dma(e, gcol[:, c:c + 1],
                                          io["gsc"][c * 128:(c + 1) * 128].rearrange("(p o) -> p o", o=1)),
              "eL", 16)
        p.run("sync", lambda e, c=c: _dma(e, ucol[:, c:c + 1],
                                          io["usc"][c * 128:(c + 1) * 128].rearrange("(p o) -> p o", o=1)),
              "eL", 16)
    for c in range(FC):
        p.run("sync", lambda e, c=c: _dma(e, dcol[:, c:c + 1],
                                          io["dsc"][c * 128:(c + 1) * 128].rearrange("(p o) -> p o", o=1)),
              "eL", 16)
    el_mark = p.mark("eL")
    p.wait("scalar", "eL", el_mark)
    p.run("scalar", lambda e: e.activation(out=gofs[:], in_=gcol[:], func=AF.Identity, scale=-128.0))
    p.run("scalar", lambda e: e.activation(out=uofs[:], in_=ucol[:], func=AF.Identity, scale=-128.0))
    p.run("scalar", lambda e: e.activation(out=dofs[:], in_=dcol[:], func=AF.Identity, scale=-128.0),
          "eOF", 1)
    of_mark = p.mark("eOF")

    for mi, (wname, oname, scol, socl) in enumerate(
            (("gw", "wgb", gcol, gofs), ("uw", "wub", ucol, uofs))):
        for c in range(64):
            r = c % 2
            p.wait("sync", "eDQ", max(0, p.cnt.get("eDQ", 0) - 1))
            p.run("sync", lambda e, c=c, r=r, wname=wname: _dma(
                e, rawr[:, r * 4096:r * 4096 + 2048].bitcast(I32) if False else
                rawr[:, r * 4096:r * 4096 + 2048],
                io[wname][c * 128:(c + 1) * 128, :]), "eR", 16)
            er_mark = p.mark("eR")
            p.wait("vector", "eR", er_mark)
            if mi > 0 or c > 0:
                p.wait("vector", "eWS", max(0, p.cnt.get("eWS", 0) - 16))
            p.wait("vector", "eOF", of_mark)
            p.run("vector", lambda e, c=c, r=r, scol=scol, socl=socl: e.tensor_scalar(
                out=deqr[:, r * 2048:(r + 1) * 2048], in0=rawr[:, r * 4096:r * 4096 + 2048],
                scalar1=scol[:, c:c + 1], scalar2=socl[:, c:c + 1],
                op0=ALU.mult, op1=ALU.add), "eDQ", 1)
            dq_mark = p.mark("eDQ")
            p.wait("gpsimd", "eDQ", dq_mark)
            p.run("gpsimd", lambda e, c=c, r=r, oname=oname: _dma(
                e, io[oname][c * 128:(c + 1) * 128, :], deqr[:, r * 2048:(r + 1) * 2048]),
                "eWS", 16)
    p.wait("sync", "eDQ", p.mark("eDQ"))
    p.wait("vector", "eWS", p.mark("eWS"))
    for c in range(FC):
        for hf in range(2):
            r = hf
            p.wait("sync", "eDQ2", max(0, p.cnt.get("eDQ2", 0) - 1))
            p.run("sync", lambda e, c=c, hf=hf, r=r: _dma(
                e, rawr[:, r * 4096:(r + 1) * 4096],
                io["dw"][c * 128:(c + 1) * 128, hf * 4096:(hf + 1) * 4096]), "eR2", 16)
            er2 = p.mark("eR2")
            p.wait("vector", "eR2", er2)
            p.wait("vector", "eWS2", max(0, p.cnt.get("eWS2", 0) - 16))
            p.run("vector", lambda e, c=c, r=r: e.tensor_scalar(
                out=deqr[:, r * 2048:r * 2048 + 2048].bitcast(BF16) if False else
                deqr[:, 0:4096] if r == 0 else deqr[:, 4096:8192],
                in0=rawr[:, r * 4096:(r + 1) * 4096],
                scalar1=dcol[:, c:c + 1], scalar2=dofs[:, c:c + 1],
                op0=ALU.mult, op1=ALU.add), "eDQ2", 1)
            dq2 = p.mark("eDQ2")
            p.wait("gpsimd", "eDQ2", dq2)
            p.run("gpsimd", lambda e, c=c, hf=hf, r=r: _dma(
                e, io["wdb"][c * 128:(c + 1) * 128, hf * 4096:(hf + 1) * 4096],
                deqr[:, 0:4096] if r == 0 else deqr[:, 4096:8192]), "eWS2", 16)
    prep_gu = p.mark("eWS")
    prep_d = p.mark("eWS2")

    B = PS
    for ic in range(8):
        i0 = ic * 1024
        if ic == 0:
            p.wait("sync", "eWS", prep_gu)
            p.wait("scalar", "eWS", prep_gu)
        p.wait("sync", "eGM", p.cnt.get("eGM", 0))
        p.wait("scalar", "eGM", p.cnt.get("eGM", 0))
        for hc in range(FC):
            p.run("sync", lambda e, hc=hc, i0=i0: e.dma_start(
                out=wgtb[:, hc * 1024:(hc + 1) * 1024],
                in_=io["wgb"][i0:i0 + 1024, hc * 128:(hc + 1) * 128], transpose=True), "eGL", 16)
            p.run("sync", lambda e, hc=hc, i0=i0: e.dma_start(
                out=wutb[:, hc * 1024:(hc + 1) * 1024],
                in_=io["wub"][i0:i0 + 1024, hc * 128:(hc + 1) * 128], transpose=True), "eGL", 16)
        gl_mark = p.mark("eGL")
        for t in range(NTT):
            tb = t % 2
            c0 = t * TT
            xn_t = xnr[:, tb * FC * TT:(tb + 1) * FC * TT]
            p.wait("sync", "eGM", max(0, p.cnt.get("eGM", 0) - 8))
            for fcx in range(FC):
                p.run("sync", lambda e, fcx=fcx, c0=c0, xn_t=xn_t: _dma(
                    e, xn_t[:, fcx * TT:(fcx + 1) * TT],
                    io["xn1"][fcx * 128:(fcx + 1) * 128, c0:c0 + TT]), "eXA", 16)
            xa_mark = p.mark("eXA")
            p.wait("tensor", "eXA", xa_mark)
            if t == 0:
                p.wait("tensor", "eGL", gl_mark)
            for sh in range(8):
                sub = sh % 4
                if ic > 0 or t > 0 or sh >= 4:
                    p.wait("tensor", "eHM", max(0, p.cnt.get("eHM", 0) - 3))
                for kc in range(FC):
                    p.run("tensor", lambda e, sh=sh, sub=sub, kc=kc, xn_t=xn_t: e.matmul(
                        B[sub][:, :], wgtb[:, kc * 1024 + sh * 128:kc * 1024 + (sh + 1) * 128],
                        xn_t[:, kc * TT:(kc + 1) * TT],
                        start=(kc == 0), stop=(kc == FC - 1)), None)
                for kc in range(FC):
                    p.run("tensor", lambda e, sh=sh, sub=sub, kc=kc, xn_t=xn_t: e.matmul(
                        B[4 + sub][:, :], wutb[:, kc * 1024 + sh * 128:kc * 1024 + (sh + 1) * 128],
                        xn_t[:, kc * TT:(kc + 1) * TT],
                        start=(kc == 0), stop=(kc == FC - 1)), "eGM", 1)
                gm_mark = p.mark("eGM")
                p.wait("scalar", "eGM", gm_mark)
                p.wait("scalar", "eHS", max(0, p.cnt.get("eHS", 0) - 3 * 16))
                p.run("scalar", lambda e, sub=sub: e.activation(
                    out=hstg[:, sub * TT:(sub + 1) * TT], in_=B[sub][:, :], func=AF.Silu), "eSG", 1)
                sg_mark = p.mark("eSG")
                p.wait("vector", "eSG", sg_mark)
                p.run("vector", lambda e, sub=sub: e.tensor_tensor(
                    out=hstg[:, sub * TT:(sub + 1) * TT], in0=B[4 + sub][:, :],
                    in1=hstg[:, sub * TT:(sub + 1) * TT], op=ALU.mult), "eHM", 1)
                hm_mark = p.mark("eHM")
                p.wait("gpsimd", "eHM", hm_mark)
                p.run("gpsimd", lambda e, sh=sh, sub=sub, i0=i0, c0=c0: _dma(
                    e, io["ha"][i0 + sh * 128:i0 + (sh + 1) * 128, c0:c0 + TT],
                    hstg[:, sub * TT:(sub + 1) * TT]), "eHS", 16)
    a_done_mm = p.mark("eGM")
    a_done_st = p.mark("eHS")

    TB = 256
    for hg in range(E_HG):
        h0 = hg * 8
        p.wait("sync", "eHS", a_done_st)
        p.wait("scalar", "eHS", a_done_st)
        if hg == 0:
            p.wait("sync", "eWS2", prep_d)
        p.wait("sync", "eGM", a_done_mm)
        p.wait("sync", "eBM", p.cnt.get("eBM", 0))
        for ic in range(64):
            p.run("sync", lambda e, ic=ic, h0=h0: e.dma_start(
                out=wdtb[:, ic * 1024:(ic + 1) * 1024],
                in_=io["wdb"][h0 * 128:h0 * 128 + 1024, ic * 128:(ic + 1) * 128],
                transpose=True), "eDL", 16)
        dl_mark = p.mark("eDL")
        for t in range(2 * E_NT):
            c0 = t * TB
            r = t % 2
            p.wait("scalar", "eBM", max(0, p.cnt.get("eBM", 0) - 8))
            for ic in range(64):
                p.run("scalar", lambda e, ic=ic, r=r, c0=c0: _dma(
                    e, hab[:, (r * 64 + ic) * TB:(r * 64 + ic + 1) * TB],
                    io["ha"][ic * 128:(ic + 1) * 128, c0:c0 + TB]), "eHL", 16)
            hl_mark = p.mark("eHL")
            p.wait("tensor", "eHL", hl_mark)
            if t == 0:
                p.wait("tensor", "eDL", dl_mark)
            for hsub in range(8):
                bank = PS[hsub % 2]
                p.wait("tensor", "eBE", max(0, p.cnt.get("eBE", 0) - 1))
                for ic in range(64):
                    p.run("tensor", lambda e, hsub=hsub, ic=ic, r=r, bank=bank: e.matmul(
                        bank[0:128, 0:TB],
                        wdtb[:, ic * 1024 + hsub * 128:ic * 1024 + (hsub + 1) * 128],
                        hab[:, (r * 64 + ic) * TB:(r * 64 + ic + 1) * TB],
                        start=(ic == 0), stop=(ic == 63)), "eBM" if ic == 63 else None)
                bm_mark = p.mark("eBM")
                ar = hsub % 2
                p.wait("sync", "eBE", max(0, p.cnt.get("eBE", 0) - 1))
                p.run("sync", lambda e, hsub=hsub, ar=ar, h0=h0, c0=c0: _dma(
                    e, apt[:, ar * TT:ar * TT + TB],
                    io["ap1"][(h0 + hsub) * 128:(h0 + hsub + 1) * 128, c0:c0 + TB]), "eAP", 16)
                ap_mark = p.mark("eAP")
                p.wait("vector", "eBM", bm_mark)
                p.wait("vector", "eAP", ap_mark)
                if p.cnt.get("eOS", 0) >= 16:
                    p.wait("vector", "eOS", max(0, p.cnt.get("eOS", 0) - 16))
                p.run("vector", lambda e, ar=ar, bank=bank: e.tensor_tensor(
                    out=ost[:, ar * TT:ar * TT + TB], in0=bank[0:128, 0:TB],
                    in1=apt[:, ar * TT:ar * TT + TB], op=ALU.add), "eBE", 1)
                be_mark = p.mark("eBE")
                p.wait("gpsimd", "eBE", be_mark)
                p.run("gpsimd", lambda e, hsub=hsub, ar=ar, h0=h0, c0=c0: _dma(
                    e, io["outfm"][(h0 + hsub) * 128:(h0 + hsub + 1) * 128, c0:c0 + TB],
                    ost[:, ar * TT:ar * TT + TB]), "eOS", 16)
    p.wait("gpsimd", "eOS")


def _common_sb(nc, p, io):
    C = {}
    C["o128"] = _sb(nc, "c_o128", (1, 128), BF16)
    C["ones_col"] = _sb(nc, "c_onesc", (128, 1), BF16)
    C["ones_r"] = _sb(nc, "c_onesr", (1, TT), BF16)
    C["ones8_r"] = _sb(nc, "c_ones8r", (1, 8), BF16)
    C["g8"] = _sb(nc, "c_g8", (64, 8), BF16)
    C["e8"] = _sb(nc, "c_e8", (8, 64), BF16)
    C["idb"] = _sb(nc, "c_idb", (128, 128), BF16)
    C["idf"] = _sb(nc, "c_idf", (128, 128), F32)
    C["cc"] = _sb(nc, "c_cc", (1, 4), F32)
    pairs = [("o128", "ones128_row_bf"), ("ones_col", "ones_col_bf"), ("ones_r", "ones_row_bf"),
             ("ones8_r", "ones8_row_bf"), ("g8", "G8_bf"), ("e8", "E8_bf"),
             ("idb", "ident_bf"), ("idf", "ident_f32"), ("cc", "cc")]
    for nm, key in pairs:
        p.run("sync", lambda e, nm=nm, key=key: _dma(e, C[nm][:], io[key][:, :]), "CL", 16)
    p.barrier("CB")
    return C


PHASE0A_INPUTS = ["x0fm", "sel_wall", "sel_muT", "sel_mu", "sel_muexp", "sel_uall", "sel_dbexp",
                  "sel_dg", "sel_bias", "imp_c", "imp_s", "ident_bf", "ident_f32", "ones_row_bf",
                  "ones128_row_bf", "ones_col_bf", "ones8_row_bf", "G8_bf", "E8_bf", "C16", "cc"]


def build_phase0a():
    nc = bass.Bass("TRN2", target_bir_lowering=False, debug=False)
    io = {}
    for nm in PHASE0A_INPUTS:
        io[nm] = _dram(nc, nm, INPUT_SPECS[nm][0], INPUT_SPECS[nm][1], "ExternalInput")
    io["xn0"] = _dram(nc, "xn0", SCRATCH["xn0"][0], SCRATCH["xn0"][1], "ExternalOutput")
    io["p0"] = _dram(nc, "p0", SCRATCH["p0"][0], SCRATCH["p0"][1], "ExternalOutput")
    if DEBUG0A:
        io["dbg"] = _dram(nc, "dbg", (33, T), F32, "ExternalOutput")
        io["dbg2"] = _dram(nc, "dbg2", (128, 72), F32, "ExternalOutput")
        io["dbg3"] = _dram(nc, "dbg3", (128, TT), F32, "ExternalOutput")
    p = Prog(nc)
    C = _common_sb(nc, p, io)
    PS = [_ps(nc, f"PSB{i}") for i in range(8)]
    emit_phase0a(p, io, C, PS)
    p.emit()
    return nc


PHASE0B_INPUTS = ["xn0_in", "l0wq", "l0wk", "l0bq", "l0bk"]


def build_phase0b():
    nc = bass.Bass("TRN2", target_bir_lowering=False, debug=False)
    io = {}
    io["xn0_in"] = _dram(nc, "xn0_in", SCRATCH["xn0"][0], SCRATCH["xn0"][1], "ExternalInput")
    for nm in ["l0wq", "l0wk", "l0bq", "l0bk"]:
        io[nm] = _dram(nc, nm, INPUT_SPECS[nm][0], INPUT_SPECS[nm][1], "ExternalInput")
    io["q0s"] = _dram(nc, "q0s", SCRATCH["q0s"][0], SCRATCH["q0s"][1], "ExternalOutput")
    io["k0s"] = _dram(nc, "k0s", SCRATCH["k0s"][0], SCRATCH["k0s"][1], "ExternalOutput")
    p = Prog(nc)
    PS = [_ps(nc, f"PSB{i}") for i in range(4)]
    emit_proj(p, io, None, PS, "xn0_in",
              [("q0s", "l0wq", "l0bq", 0, 1, None, False),
               ("k0s", "l0wk", "l0bk", 2, 3, None, False)], "b")
    p.emit()
    return nc


COMMON_CONST_INPUTS = ["ident_bf", "ident_f32", "ones_row_bf", "ones128_row_bf",
                       "ones_col_bf", "ones8_row_bf", "G8_bf", "E8_bf", "cc"]

PHASE0C_INPUTS = ["xn0_in", "q0s_in", "k0s_in", "p0_in", "l0wv", "l0bv",
                  "dt_bf", "cs_f32", "kv_f32"] + COMMON_CONST_INPUTS


def build_phase0c():
    nc = bass.Bass("TRN2", target_bir_lowering=False, debug=False)
    io = {}
    io["xn0"] = _dram(nc, "xn0_in", SCRATCH["xn0"][0], SCRATCH["xn0"][1], "ExternalInput")
    io["q0s"] = _dram(nc, "q0s_in", SCRATCH["q0s"][0], SCRATCH["q0s"][1], "ExternalInput")
    io["k0s"] = _dram(nc, "k0s_in", SCRATCH["k0s"][0], SCRATCH["k0s"][1], "ExternalInput")
    io["p0"] = _dram(nc, "p0_in", SCRATCH["p0"][0], SCRATCH["p0"][1], "ExternalInput")
    for nm in ["l0wv", "l0bv", "dt_bf", "cs_f32", "kv_f32"] + COMMON_CONST_INPUTS:
        io[nm] = _dram(nc, nm, INPUT_SPECS[nm][0], INPUT_SPECS[nm][1], "ExternalInput")
    io["x1"] = _dram(nc, "x1", SCRATCH["x1"][0], SCRATCH["x1"][1], "ExternalOutput")
    p = Prog(nc)
    C = _common_sb(nc, p, io)
    PS = [_ps(nc, f"PSB{i}") for i in range(5)]
    psbf = _ps(nc, "PSBF", (128, 1024), BF16)
    emit_phase0c(p, io, C, PS, psbf)
    p.emit()
    return nc


PHASE1A_INPUTS = ["x1_in"] + COMMON_CONST_INPUTS


def build_phase1a():
    nc = bass.Bass("TRN2", target_bir_lowering=False, debug=False)
    io = {}
    io["x1"] = _dram(nc, "x1_in", SCRATCH["x1"][0], SCRATCH["x1"][1], "ExternalInput")
    for nm in COMMON_CONST_INPUTS:
        io[nm] = _dram(nc, nm, INPUT_SPECS[nm][0], INPUT_SPECS[nm][1], "ExternalInput")
    io["xn1"] = _dram(nc, "xn1", SCRATCH["xn1"][0], SCRATCH["xn1"][1], "ExternalOutput")
    p = Prog(nc)
    C = _common_sb(nc, p, io)
    PS = [_ps(nc, f"PSB{i}") for i in range(8)]
    emit_ln(p, io, C, PS, "x1", "xn1")
    p.emit()
    return nc


PHASE1B_INPUTS = ["xn1_in", "l1wq", "l1wk", "l1bq", "l1bk", "beosT", "ustrict"] + COMMON_CONST_INPUTS


def build_phase1b():
    nc = bass.Bass("TRN2", target_bir_lowering=False, debug=False)
    io = {}
    io["xn1"] = _dram(nc, "xn1_in", SCRATCH["xn1"][0], SCRATCH["xn1"][1], "ExternalInput")
    for nm in ["l1wq", "l1wk", "l1bq", "l1bk", "beosT", "ustrict"] + COMMON_CONST_INPUTS:
        io[nm] = _dram(nc, nm, INPUT_SPECS[nm][0], INPUT_SPECS[nm][1], "ExternalInput")
    for nm in ["q1s", "k1s", "q1t", "nrm", "msk", "rat", "ra"]:
        io[nm] = _dram(nc, nm, SCRATCH[nm][0], SCRATCH[nm][1], "ExternalOutput")
    p = Prog(nc)
    C = _common_sb(nc, p, io)
    PS = [_ps(nc, f"PSB{i}") for i in range(6)]
    psbf = _ps(nc, "PSBF", (128, 1024), BF16)
    nrm_cols = emit_proj(p, io, C, PS, "xn1",
                         [("q1s", "l1wq", "l1bq", 0, 1, "q1t", True),
                          ("k1s", "l1wk", "l1bk", 2, 3, None, False)], "q", psbf=psbf)
    emit_sel64(p, io, C, [None, None, None, None, None, PS[5]], nrm_cols)
    p.emit()
    return nc


PHASE1C_INPUTS = ["xn1_in", "l1wv", "l1bv"] + COMMON_CONST_INPUTS


def build_phase1c():
    nc = bass.Bass("TRN2", target_bir_lowering=False, debug=False)
    io = {}
    io["xn1"] = _dram(nc, "xn1_in", SCRATCH["xn1"][0], SCRATCH["xn1"][1], "ExternalInput")
    for nm in ["l1wv", "l1bv"] + COMMON_CONST_INPUTS:
        io[nm] = _dram(nc, nm, INPUT_SPECS[nm][0], INPUT_SPECS[nm][1], "ExternalInput")
    for nm in ["v1s", "v1t"]:
        pass
    io["v1s"] = _dram(nc, "v1s", (D, T), BF16, "ExternalOutput")
    io["v1t"] = _dram(nc, "v1t", SCRATCH["v1t"][0], SCRATCH["v1t"][1], "ExternalOutput")
    p = Prog(nc)
    C = _common_sb(nc, p, io)
    PS = [_ps(nc, f"PSB{i}") for i in range(2)]
    psbf = _ps(nc, "PSBF", (128, 1024), BF16)
    emit_proj(p, io, C, PS, "xn1",
              [("v1s", "l1wv", "l1bv", 0, 1, "v1t", False)], "v", psbf=psbf)
    p.emit()
    return nc


PHASE1D_INPUTS = ["q1s_in", "q1t_in", "k1s_in", "v1t_in", "rat_in", "ra_in", "x1_in",
                  "ones64_row", "iota64_col", "iota64_bc"] + COMMON_CONST_INPUTS


def build_phase1d():
    nc = bass.Bass("TRN2", target_bir_lowering=False, debug=False)
    io = {}
    for nm, sc in (("q1s", "q1s"), ("q1t", "q1t"), ("k1s", "k1s"), ("v1t", "v1t"),
                   ("rat", "rat"), ("ra", "ra"), ("x1", "x1")):
        io[nm] = _dram(nc, nm + "_in", SCRATCH[sc][0], SCRATCH[sc][1], "ExternalInput")
    for nm in ["ones64_row", "iota64_col", "iota64_bc"] + COMMON_CONST_INPUTS:
        io[nm] = _dram(nc, nm, INPUT_SPECS[nm][0], INPUT_SPECS[nm][1], "ExternalInput")
    io["ap1"] = _dram(nc, "ap1", SCRATCH["ap1"][0], SCRATCH["ap1"][1], "ExternalOutput")
    if DEBUG1D:
        io["dbg1"] = _dram(nc, "dbg1", (256, 260), F32, "ExternalOutput")
        io["dbg2"] = _dram(nc, "dbg2", (128, 640), F32, "ExternalOutput")
        io["dbg3"] = _dram(nc, "dbg3", (128, 256), F32, "ExternalOutput")
        io["dbg4"] = _dram(nc, "dbg4", (128, 384), BF16, "ExternalOutput")
        io["dbg5"] = _dram(nc, "dbg5", (128, 256), F32, "ExternalOutput")
    p = Prog(nc)
    C = _common_sb(nc, p, io)
    PS = [_ps(nc, f"PSB{i}") for i in range(6)]
    psbf = _ps(nc, "PSBF", (128, 1024), BF16)
    emit_phase1d(p, io, C, PS, psbf)
    p.emit()
    return nc


PHASE1E_INPUTS = ["xn1_in", "ap1_in", "gw", "uw", "dw", "gsc", "usc", "dsc"] + COMMON_CONST_INPUTS


def build_phase1e():
    nc = bass.Bass("TRN2", target_bir_lowering=False, debug=False)
    io = {}
    io["xn1"] = _dram(nc, "xn1_in", SCRATCH["xn1"][0], SCRATCH["xn1"][1], "ExternalInput")
    io["ap1"] = _dram(nc, "ap1_in", SCRATCH["ap1"][0], SCRATCH["ap1"][1], "ExternalInput")
    for nm in ["gw", "uw", "dw", "gsc", "usc", "dsc"] + COMMON_CONST_INPUTS:
        io[nm] = _dram(nc, nm, INPUT_SPECS[nm][0], INPUT_SPECS[nm][1], "ExternalInput")
    for nm in ["wgb", "wub", "wdb", "ha"]:
        io[nm] = _dram(nc, nm, SCRATCH[nm][0], SCRATCH[nm][1],
                       "ExternalOutput" if DEBUG1E else "Internal")
    io["outfm"] = _dram(nc, "outfm", SCRATCH["outfm"][0], SCRATCH["outfm"][1], "ExternalOutput")
    p = Prog(nc)
    C = _common_sb(nc, p, io)
    PS = [_ps(nc, f"PSB{i}") for i in range(8)]
    emit_phase1e(p, io, C, PS)
    p.emit()
    return nc


def kernel(**inputs):
    per_core = host_prep(inputs)
    state = [dict(pc) for pc in per_core]

    def launch(build_fn, in_names, out_map):
        nc = build_fn()
        in_maps = []
        for st in state:
            m = {}
            for nm in in_names:
                src_nm = nm[:-3] if nm.endswith("_in") else nm
                v = st[src_nm]
                m[nm] = v
            in_maps.append(m)
        res = run_bass_kernel_spmd(nc, in_maps, core_ids=list(range(NCORES)))
        for ci, st in enumerate(state):
            for nm in out_map:
                st[nm] = np.asarray(res.results[ci][nm])

    launch(build_phase0a, PHASE0A_INPUTS, ["xn0", "p0"])
    launch(build_phase0b, ["xn0_in", "l0wq", "l0wk", "l0bq", "l0bk"], ["q0s", "k0s"])
    launch(build_phase0c, PHASE0C_INPUTS, ["x1"])
    launch(build_phase1a, PHASE1A_INPUTS, ["xn1"])
    launch(build_phase1b, PHASE1B_INPUTS, ["q1s", "k1s", "q1t", "nrm", "msk", "rat", "ra"])
    launch(build_phase1c, PHASE1C_INPUTS, ["v1s", "v1t"])
    launch(build_phase1d, PHASE1D_INPUTS, ["ap1"])
    launch(build_phase1e, PHASE1E_INPUTS, ["outfm"])

    out = np.stack([np.ascontiguousarray(st["outfm"].T) for st in state], axis=0)
    return out.astype(np.float32)
